# version 05
# baseline (speedup 1.0000x reference)
import numpy as np
import ml_dtypes
from contextlib import ExitStack

import concourse.bass as bass
import concourse.mybir as mybir
from concourse.bass_utils import run_bass_kernel_spmd

F32 = mybir.dt.float32
BF16 = mybir.dt.bfloat16
AF = mybir.ActivationFunctionType
ALU = mybir.AluOpType

NCORE = 8
T = 1024
D = 2048
DCH = 16
NPAIR = 64
GPP = 4
NGRP = NPAIR // GPP
LN_EPS = 1e-5
RMS_EPS = float(np.finfo(np.float32).eps)

bf = ml_dtypes.bfloat16

_DEBUG_TAPS = []


class Ctr:
    def __init__(self, h):
        self.h = h
        self.v = 0

    def inc(self, inst, n=1):
        inst.then_inc(self.h, n)
        self.v += n
        return self.v


def build_program():
    nc = bass.Bass("TRN2", target_bir_lowering=False, detect_race_conditions=False)

    xT = nc.dram_tensor("xT", [DCH, 128, T], F32, kind="ExternalInput")
    xrT = nc.dram_tensor("xrT", [DCH, 128, T], F32, kind="ExternalInput")
    wsT = nc.dram_tensor("wsT", [16, 128, 2048], F32, kind="ExternalInput")
    wkT = nc.dram_tensor("wkT", [128, 1024], F32, kind="ExternalInput")
    wvT = nc.dram_tensor("wvT", [128, 1024], F32, kind="ExternalInput")
    woT = nc.dram_tensor("woT", [64, 2048], F32, kind="ExternalInput")
    gcols = nc.dram_tensor("gcols", [128, 16], F32, kind="ExternalInput")
    lgcols = nc.dram_tensor("lgcols", [128, 16], F32, kind="ExternalInput")
    lbcols = nc.dram_tensor("lbcols", [128, 16], F32, kind="ExternalInput")
    tdc = nc.dram_tensor("tdc", [64, 1], F32, kind="ExternalInput")
    tdr = nc.dram_tensor("tdr", [1, 64], F32, kind="ExternalInput")
    tfc = nc.dram_tensor("tfc", [64, 1], F32, kind="ExternalInput")
    expo = nc.dram_tensor("expo", [1, 8], F32, kind="ExternalInput")
    indst = nc.dram_tensor("indst", [128, 2], BF16, kind="ExternalInput")
    indbf = nc.dram_tensor("indbf", [2, 128], F32, kind="ExternalInput")
    indbb = nc.dram_tensor("indbb", [2, 128], BF16, kind="ExternalInput")
    qgu = nc.dram_tensor("qgu", [16, 128, 2048], F32, kind="ExternalInput")
    qpj = nc.dram_tensor("qpj", [8, 128, 2048], F32, kind="ExternalInput")

    out_d = nc.dram_tensor("out", [DCH, 128, T], F32, kind="ExternalOutput")
    taps = {}
    for name in _DEBUG_TAPS:
        shp = {"xn": [DCH, 128, T], "xnorm": [DCH, 128, T], "wkv": [64, T],
               "states": [64, T], "F": [DCH, 128, T],
               "sseg": [8, 64], "sE": [8, 64], "sP": [8, 64], "sGv": [64, 1],
               "sa": [64, T], "stash": [64, 1], "wkv2": [64, T],
               "xs": [DCH, 128, T], "xnorm": [DCH, 128, T]}[name]
        taps[name] = nc.dram_tensor("tap_" + name, shp, F32, kind="ExternalOutput")

    xs_d = nc.dram_tensor("xs_d", [DCH, 128, T], F32)
    oa_d = nc.dram_tensor("oa_d", [DCH, 128, T], F32)
    seg_d = nc.dram_tensor("seg_d", [64, 1], F32)
    segall = nc.dram_tensor("segall", [8, 64], F32)
    wqgu_sh = nc.dram_tensor("wqgu_sh", [16, 128, 16, 128], BF16)
    wqgu_all = nc.dram_tensor("wqgu_all", [8, 16, 128, 16, 128], BF16)
    wqpj_sh = nc.dram_tensor("wqpj_sh", [8, 128, 16, 128], BF16)
    wqpj_all = nc.dram_tensor("wqpj_all", [8, 8, 128, 16, 128], BF16)

    with ExitStack() as st_:
        def sbuf(name, shape, dtype):
            return st_.enter_context(nc.sbuf_tensor(name, shape, dtype))

        AR1 = sbuf("AR1", [128, DCH * T], F32)
        AR2 = sbuf("AR2", [128, DCH * T], F32)
        AR3 = sbuf("AR3", [128, 4096], F32)
        WARE = sbuf("WARE", [128, 8704], F32)
        S0 = sbuf("S0", [128, T], F32)
        S1 = sbuf("S1", [128, T], F32)
        S2 = sbuf("S2", [128, T], F32)
        S3 = sbuf("S3", [128, T], F32)
        S4 = sbuf("S4", [128, T], F32)
        sgc = sbuf("sgc", [128, 16], F32)
        somg = sbuf("somg", [128, 16], F32)
        slg = sbuf("slg", [128, 16], F32)
        slb = sbuf("slb", [128, 16], F32)
        stdc = sbuf("stdc", [64, 1], F32)
        stdr = sbuf("stdr", [1, 64], F32)
        stfc = sbuf("stfc", [64, 1], F32)
        sexpo = sbuf("sexpo", [1, 8], F32)
        sindst = sbuf("sindst", [128, 2], BF16)
        sindbf = sbuf("sindbf", [2, 128], F32)
        sindbb = sbuf("sindbb", [2, 128], BF16)
        su = sbuf("su", [64, 1], F32)
        sw1 = sbuf("sw1", [64, 1], F32)
        sseg = sbuf("sseg", [8, 64], F32)
        sE = sbuf("sE", [8, 64], F32)
        sP = sbuf("sP", [8, 64], F32)
        sGv = sbuf("sGv", [64, 1], F32)
        szero = sbuf("szero", [64, 1], F32)
        sstash = sbuf("sstash", [64, 1], F32)
        sones8 = sbuf("sones8", [8, 1], F32)
        sones128 = sbuf("sones128", [128, 1], F32)
        sonesrow = sbuf("sonesrow", [1, 128], F32)

        PS1 = st_.enter_context(nc.psum_tensor("PS1", [128, 1024], F32))
        PS2 = st_.enter_context(nc.psum_tensor("PS2", [128, 1024], F32))
        PS3 = st_.enter_context(nc.psum_tensor("PS3", [128, 1024], F32))
        PS4 = st_.enter_context(nc.psum_tensor("PS4", [128, 1024], F32))

        qL = Ctr(st_.enter_context(nc.semaphore("qL")))
        qA = Ctr(st_.enter_context(nc.semaphore("qA")))
        qG = Ctr(st_.enter_context(nc.semaphore("qG")))
        cT = Ctr(st_.enter_context(nc.semaphore("cT")))
        cV = Ctr(st_.enter_context(nc.semaphore("cV")))
        cS = Ctr(st_.enter_context(nc.semaphore("cS")))
        cG = Ctr(st_.enter_context(nc.semaphore("cG")))
        cc = Ctr(st_.enter_context(nc.semaphore("cc")))

        block = st_.enter_context(nc.Block())

        AR1b = AR1.ap().bitcast(BF16)
        AR3b = AR3.ap().bitcast(BF16)
        WARb = WARE.ap().bitcast(BF16)
        S0b = S0.ap().bitcast(BF16)
        S1b = S1.ap().bitcast(BF16)

        w_wbf = [WARb[:, 0:2048], WARb[:, 2048:4096]]
        w_wabs = WARb[:, 4096:6144]
        w_rq = WARb[:, 6144:7168]
        w_d1 = WARb[:, 7168:8192]
        w_d2 = WARb[:, 8192:9216]
        w_dd = WARb[:, 9216:10240]
        w_wqb = WARb[:, 10240:12288]
        a_scf = WARE.ap()[0:2, 6144:7168]
        a_rs = WARE.ap()[0:2, 7168:8192]
        sscbf = WARb[0:2, 16384:17408]
        WS0 = AR3.ap()[:, 0:2048]
        WS1 = AR3.ap()[:, 2048:4096]

        R_mu = S3.ap()[0:1, 0:1024]
        R_rs = S4.ap()[0:1, 0:1024]
        tmpB = AR3.ap()[0:1, 0:1024]

        def srow(t):
            return t.ap()[0:1, 0:1024]

        mil = {}

        def ld(e, q, out, in_):
            q.inc(e.dma_start(out=out, in_=in_), 16)
            return q.v

        def _(e):
            ld(e, qL, sgc[:, :], gcols[:, :])
            ld(e, qL, slg[:, :], lgcols[:, :])
            ld(e, qL, slb[:, :], lbcols[:, :])
            ld(e, qL, stdc[:, :], tdc[:, :])
            ld(e, qL, stdr[:, :], tdr[:, :])
            ld(e, qL, stfc[:, :], tfc[:, :])
            ld(e, qL, sexpo[:, :], expo[:, :])
            ld(e, qL, sindst[:, :], indst[:, :])
            ld(e, qL, sindbf[:, :], indbf[:, :])
            mil['smalls'] = ld(e, qL, sindbb[:, :], indbb[:, :])
        block.sync(_)

        def _(e):
            e.memset(szero[:, :], 0.0)
            e.memset(sones8[:, :], 1.0)
            e.memset(sones128[:, :], 1.0)
            cV.inc(e.memset(sonesrow[:, :], 1.0))
            mil['consts'] = cV.v
        block.vector(_)

        def _(e):
            e.wait_ge(qL.h, mil['smalls'])
            cS.inc(e.activation(sgc[:, :], sgc[:, :], AF.Sigmoid))
            mil['gsig'] = cS.v
        block.scalar(_)

        def _(e):
            e.wait_ge(cS.h, mil['gsig'])
            cV.inc(e.tensor_scalar(somg[:, :], sgc[:, :], -1.0, 1.0, ALU.mult, ALU.add))
            mil['gates'] = cV.v
        block.vector(_)

        def _(e):
            e.wait_ge(qL.h, mil['smalls'])
            e.activation(su[:, :], stfc[:, :], AF.Exp)
            cS.inc(e.activation(sw1[:, :], stdc[:, :], AF.Exp))
            mil['uw'] = cS.v
        block.scalar(_)

        def q_src(i):
            return qgu.ap()[i, :, :] if i < 16 else qpj.ap()[i - 16, :, :]

        def q_dst(i):
            if i < 16:
                return wqgu_sh.ap()[:, :, i, :].transpose([1, 0, 2])
            return wqpj_sh.ap()[i - 16, :, :, :]

        NQ = 24
        q_cast = [None] * (NQ + 1)
        q_abs = [None] * NQ
        q_stats = [[None, None] for _ in range(NQ)]
        q_scale = [[None, None] for _ in range(NQ)]
        q_bc = [[None, None] for _ in range(NQ)]
        q_r = [[None, None] for _ in range(NQ)]
        q_dd = [[None, None] for _ in range(NQ)]
        q_wq = [[None, None] for _ in range(NQ)]
        q_wr = [None] * NQ

        def _(e):
            q_cast[0] = ld(e, qG, w_wbf[0], q_src(0))
        block.gpsimd(_)

        def emit_qtile(i):
            buf = w_wbf[i % 2]

            def _(e):
                if i + 1 < NQ:
                    if i >= 1:
                        e.wait_ge(cV.h, q_r[i - 1][1])
                    q_cast[i + 1] = ld(e, qG, w_wbf[(i + 1) % 2], q_src(i + 1))
            block.gpsimd(_)

            def _(e):
                e.wait_ge(qG.h, q_cast[i])
                if i >= 1:
                    e.wait_ge(cT.h, q_stats[i - 1][1])
                q_abs[i] = cS.inc(e.activation(w_wabs, buf, AF.Abs))
            block.scalar(_)

            for h in range(2):
                cols = slice(h * 1024, (h + 1) * 1024)

                def _(e, h=h):
                    if h == 0:
                        e.wait_ge(cS.h, q_abs[i])
                        if i >= 1:
                            e.wait_ge(cV.h, q_wq[i - 1][1])
                    else:
                        e.wait_ge(cV.h, q_wq[i][0])
                    c0 = h * 1024
                    e.matmul(PS4[0:2, 0:512], sindst[:, :], w_wabs[:, c0:c0 + 512],
                             start=True, stop=True)
                    q_stats[i][h] = cT.inc(e.matmul(
                        PS4[0:2, 512:1024], sindst[:, :], w_wabs[:, c0 + 512:c0 + 1024],
                        start=True, stop=True))
                block.tensor(_)

                def _(e, h=h):
                    e.wait_ge(cT.h, q_stats[i][h])
                    e.tensor_scalar(a_scf, PS4[0:2, 0:1024], 1.0 / 64, 1e-8, ALU.mult, ALU.max)
                    e.tensor_copy(sscbf, a_scf)
                    q_scale[i][h] = cV.inc(e.reciprocal(a_rs, sscbf))
                block.vector(_)

                def _(e, h=h):
                    e.wait_ge(cV.h, q_scale[i][h])
                    prev_r = q_r[i][0] if h == 1 else (q_r[i - 1][1] if i >= 1 else None)
                    if prev_r is not None:
                        e.wait_ge(cV.h, prev_r)
                    e.matmul(PS3[:, 0:512], sindbf[:, :], a_rs[:, 0:512], start=True, stop=True)
                    e.matmul(PS3[:, 512:1024], sindbf[:, :], a_rs[:, 512:1024], start=True, stop=True)
                    e.matmul(PS4[:, 0:512], sindbb[:, :], sscbf[:, 0:512],
                             start=True, stop=True)
                    q_bc[i][h] = cT.inc(e.matmul(
                        PS4[:, 512:1024], sindbb[:, :], sscbf[:, 512:1024],
                        start=True, stop=True))
                block.tensor(_)

                def _(e, h=h, cols=cols):
                    e.wait_ge(cT.h, q_bc[i][h])
                    q_r[i][h] = cV.inc(e.tensor_tensor(w_rq, buf[:, cols], PS3[:, :], ALU.mult))
                block.vector(_)

                def _(e, h=h):
                    e.wait_ge(cV.h, q_r[i][h])
                    e.tensor_single_scalar(w_d1, w_rq, 0.5, ALU.is_gt)
                    e.tensor_single_scalar(w_d2, w_rq, -0.5, ALU.is_lt)
                    q_dd[i][h] = cG.inc(e.tensor_tensor(w_dd, w_d1, w_d2, ALU.subtract))
                block.gpsimd(_)

                def _(e, h=h, cols=cols):
                    e.wait_ge(cG.h, q_dd[i][h])
                    if h == 0 and i >= 1:
                        e.wait_ge(qA.h, q_wr[i - 1])
                    q_wq[i][h] = cV.inc(e.tensor_tensor(
                        w_wqb[:, cols], w_dd, PS4[:, :], ALU.mult))
                block.vector(_)

            def _(e):
                e.wait_ge(cV.h, q_wq[i][1])
                src = w_wqb.rearrange("p (a f) -> p a f", f=128)
                q_wr[i] = ld(e, qA, q_dst(i), src)
            block.scalar(_)

        l1_ld = [None] * DCH
        l1_sq = [None] * DCH
        l1_mm = [None] * DCH

        def emit_l1_stats(dc, wait_v_first=None):
            sq_buf = S3 if dc % 2 == 0 else S4

            def _(e):
                e.wait_ge(qL.h, l1_ld[dc])
                if dc >= 2:
                    e.wait_ge(cT.h, l1_mm[dc - 2])
                l1_sq[dc] = cS.inc(e.square(sq_buf[:, :], AR2[:, dc * T:(dc + 1) * T]))
            block.scalar(_)

            def _(e):
                e.wait_ge(cS.h, l1_sq[dc])
                if dc == 0 and wait_v_first is not None:
                    e.wait_ge(cV.h, wait_v_first)
                ch = AR2.ap()[:, dc * T:(dc + 1) * T]
                e.matmul(PS1[0:1, 0:512], sones128[:, :], ch[:, 0:512],
                         start=(dc == 0), stop=(dc == DCH - 1))
                e.matmul(PS1[0:1, 512:1024], sones128[:, :], ch[:, 512:1024],
                         start=(dc == 0), stop=(dc == DCH - 1))
                e.matmul(PS2[0:1, 0:512], sones128[:, :], sq_buf[:, 0:512],
                         start=(dc == 0), stop=(dc == DCH - 1))
                l1_mm[dc] = cT.inc(e.matmul(
                    PS2[0:1, 512:1024], sones128[:, :], sq_buf[:, 512:1024],
                    start=(dc == 0), stop=(dc == DCH - 1)))
            block.tensor(_)

        def emit_rows_finalize(sx_row, sx2_row, tag, eps=LN_EPS, is_rms=False, nmean=None,
                               wait_t=None):
            tmpA = srow(S2)

            def _(e):
                if wait_t is not None:
                    e.wait_ge(cT.h, wait_t())
                if is_rms:
                    e.tensor_scalar(R_mu, sx_row, 1.0 / nmean, eps, ALU.mult, ALU.add)
                    e.tensor_copy(R_rs, R_mu)
                else:
                    e.tensor_scalar_mul(R_mu, sx_row, 1.0 / nmean)
                    e.tensor_scalar_mul(R_rs, sx2_row, 1.0 / nmean)
                    e.tensor_tensor(tmpA, R_mu, R_mu, ALU.mult)
                    e.tensor_tensor(R_rs, R_rs, tmpA, ALU.subtract)
                    e.tensor_scalar_add(R_rs, R_rs, eps)
                cV.inc(e.reciprocal(tmpA, R_rs))
                mil[tag + '_rc'] = cV.v
            block.vector(_)

            def _(e):
                e.wait_ge(cV.h, mil[tag + '_rc'])
                cS.inc(e.sqrt(tmpA, tmpA))
                mil[tag + '_y0'] = cS.v
            block.scalar(_)

            def _(e):
                e.wait_ge(cS.h, mil[tag + '_y0'])
                e.tensor_tensor(tmpB, tmpA, tmpA, ALU.mult)
                e.tensor_tensor(tmpB, tmpB, R_rs, ALU.mult)
                e.tensor_scalar(tmpB, tmpB, -0.5, 1.5, ALU.mult, ALU.add)
                cV.inc(e.tensor_tensor(R_rs, tmpA, tmpB, ALU.mult))
                mil[tag + '_rows'] = cV.v
            block.vector(_)

        def emit_ln_finalize(eps, tag):
            emit_rows_finalize(PS1[0:1, 0:1024], PS2[0:1, 0:1024], tag, eps=eps, nmean=D,
                               wait_t=lambda: l1_mm[DCH - 1])

            def _(e):
                e.wait_ge(cV.h, mil[tag + '_rows'])
                e.matmul(PS1[:, 0:512], sonesrow[:, :], R_mu[:, 0:512], start=True, stop=True)
                e.matmul(PS1[:, 512:1024], sonesrow[:, :], R_mu[:, 512:1024], start=True, stop=True)
                e.matmul(PS2[:, 0:512], sonesrow[:, :], R_rs[:, 0:512], start=True, stop=True)
                mil[tag + '_bc'] = cT.inc(e.matmul(
                    PS2[:, 512:1024], sonesrow[:, :], R_rs[:, 512:1024], start=True, stop=True))
            block.tensor(_)

        def emit_xn_pass(dst, tag):
            for dc in range(DCH):
                def _(e, dc=dc):
                    if dc == 0:
                        e.wait_ge(cT.h, mil[tag + '_bc'])
                    ch_in = AR2.ap()[:, dc * T:(dc + 1) * T]
                    e.tensor_tensor(S2[:, :], ch_in, PS1[:, :], ALU.subtract)
                    cV.inc(e.tensor_tensor(dst[:, dc * T:(dc + 1) * T], S2[:, :], PS2[:, :], ALU.mult))
                    mil[tag + '_n%d' % dc] = cV.v
                block.vector(_)

        for dc in range(DCH):
            def _(e, dc=dc):
                l1_ld[dc] = ld(e, qL, AR2[:, dc * T:(dc + 1) * T], xT.ap()[dc, :, :])
            block.sync(_)
            emit_l1_stats(dc)
        emit_ln_finalize(LN_EPS, 'ln1x')
        emit_xn_pass(AR1.ap(), 'ln1x')

        for dc in range(DCH):
            def _(e, dc=dc):
                e.wait_ge(cV.h, mil['ln1x_n%d' % dc])
                l1_ld[dc] = ld(e, qL, AR2[:, dc * T:(dc + 1) * T], xrT.ap()[dc, :, :])
            block.sync(_)
            emit_l1_stats(dc, wait_v_first=mil['ln1x_n%d' % (DCH - 1)] if dc == 0 else None)
        emit_ln_finalize(LN_EPS, 'ln1r')
        emit_xn_pass(AR2.ap(), 'ln1r')

        qsched = {}
        for _qi in range(16):
            emit_qtile(_qi)

        def _(e):
            e.wait_ge(qA.h, q_wr[15])
            mil['ag_gu'] = cc.inc(e.collective_compute(
                "AllGather", ALU.bypass, replica_groups=[list(range(NCORE))],
                ins=[wqgu_sh.ap()], outs=[wqgu_all.ap()]))
        block.gpsimd(_)
        for _qi in range(16, 24):
            emit_qtile(_qi)

        def _(e):
            e.wait_ge(qA.h, q_wr[23])
            mil['ag_pj'] = cc.inc(e.collective_compute(
                "AllGather", ALU.bypass, replica_groups=[list(range(NCORE))],
                ins=[wqpj_sh.ap()], outs=[wqpj_all.ap()]))
        block.gpsimd(_)

        def _(e):
            e.wait_ge(cV.h, mil['ln1r_rows'])
            e.wait_ge(cc.h, 2)
            mil['ws0'] = ld(e, qL, WS0, wsT.ap()[0, :, :])
        block.sync(_)

        sh_mm = [None] * 16
        sh_xs = [None] * 16
        sh_ev = [None] * 16
        sh_sq = [None] * 16
        sh_x2 = [None] * 16
        sh_st = [None] * 16
        for ec in range(16):
            def _(e, ec=ec):
                if ec + 1 < 16:
                    if ec >= 1:
                        e.wait_ge(cT.h, sh_mm[ec - 1])
                    mil['ws%d' % (ec + 1)] = ld(
                        e, qL, (WS1 if (ec + 1) % 2 else WS0), wsT.ap()[ec + 1, :, :])
            block.sync(_)

            def _(e, ec=ec):
                e.wait_ge(qL.h, mil['ws%d' % ec])
                if ec == 0:
                    e.wait_ge(cV.h, mil['ln1r_n%d' % (DCH - 1)])
                if ec >= 2:
                    e.wait_ge(cV.h, sh_xs[ec - 2])
                ps = PS1 if ec % 2 == 0 else PS2
                wb = WS0 if ec % 2 == 0 else WS1
                last = None
                for tb in range(2):
                    for dc in range(DCH):
                        last = e.matmul(ps[:, tb * 512:(tb + 1) * 512],
                                        wb[:, dc * 128:(dc + 1) * 128],
                                        AR2[:, dc * T + tb * 512: dc * T + (tb + 1) * 512],
                                        start=(dc == 0), stop=(dc == DCH - 1))
                sh_mm[ec] = cT.inc(last)
            block.tensor(_)

            def _(e, ec=ec):
                e.wait_ge(cT.h, sh_mm[ec])
                if ec == 0:
                    e.wait_ge(cV.h, mil['gates'])
                if ec >= 2:
                    e.wait_ge(qL.h, sh_st[ec - 2])
                ps = PS1 if ec % 2 == 0 else PS2
                bxs = S0 if ec % 2 == 0 else S1
                e.tensor_scalar_mul(S2[:, :], AR1[:, ec * T:(ec + 1) * T], somg[:, ec:ec + 1])
                sh_xs[ec] = cV.inc(e.scalar_tensor_tensor(
                    bxs[:, :], ps[:, :], sgc[:, ec:ec + 1], S2[:, :], ALU.mult, ALU.add))
                if ec == 0:
                    sh_ev[ec] = cV.inc(e.tensor_copy(S3[:, :], bxs[:, :]))
                else:
                    sh_ev[ec] = cV.inc(e.tensor_tensor(S3[:, :], S3[:, :], bxs[:, :], ALU.add))
            block.vector(_)

            def _(e, ec=ec):
                e.wait_ge(cV.h, sh_xs[ec])
                bxs = S0 if ec % 2 == 0 else S1
                sh_sq[ec] = cS.inc(e.square(S2[:, :], bxs[:, :]))
            block.scalar(_)

            def _(e, ec=ec):
                e.wait_ge(cS.h, sh_sq[ec])
                if ec == 0:
                    sh_x2[ec] = cV.inc(e.tensor_copy(S4[:, :], S2[:, :]))
                else:
                    sh_x2[ec] = cV.inc(e.tensor_tensor(S4[:, :], S4[:, :], S2[:, :], ALU.add))
            block.vector(_)

            def _(e, ec=ec):
                e.wait_ge(cV.h, sh_xs[ec])
                bxs = S0 if ec % 2 == 0 else S1
                sh_st[ec] = ld(e, qL, xs_d.ap()[ec, :, :], bxs[:, :])
            block.sync(_)

            for qi in qsched.get(ec, []):
                emit_qtile(qi)

        def _(e):
            e.wait_ge(cV.h, q_wq[23][1])
            e.wait_ge(cV.h, sh_x2[15])
            e.matmul(PS3[0:1, 0:512], sones128[:, :], S3[:, 0:512], start=True, stop=True)
            e.matmul(PS3[0:1, 512:1024], sones128[:, :], S3[:, 512:1024], start=True, stop=True)
            e.matmul(PS4[0:1, 0:512], sones128[:, :], S4[:, 0:512], start=True, stop=True)
            mil['l2_sums'] = cT.inc(e.matmul(
                PS4[0:1, 512:1024], sones128[:, :], S4[:, 512:1024], start=True, stop=True))
        block.tensor(_)

        emit_rows_finalize(PS3[0:1, 0:1024], PS4[0:1, 0:1024], 'l2', eps=LN_EPS, nmean=D,
                           wait_t=lambda: mil['l2_sums'])

        def _(e):
            e.wait_ge(cV.h, mil['l2_rows'])
            e.wait_ge(cV.h, sh_xs[15])
            e.matmul(PS1[:, 0:512], sonesrow[:, :], R_mu[:, 0:512], start=True, stop=True)
            e.matmul(PS1[:, 512:1024], sonesrow[:, :], R_mu[:, 512:1024], start=True, stop=True)
            e.matmul(PS2[:, 0:512], sonesrow[:, :], R_rs[:, 0:512], start=True, stop=True)
            mil['l2_bc'] = cT.inc(e.matmul(
                PS2[:, 512:1024], sonesrow[:, :], R_rs[:, 512:1024], start=True, stop=True))
        block.tensor(_)

        def _(e):
            e.wait_ge(cV.h, q_wq[23][1])
            e.wait_ge(cG.h, q_dd[23][1])
            e.wait_ge(qA.h, q_wr[23])
            ld(e, qL, WARE[:, 0:1024], wkT[:, :])
            ld(e, qL, WARE[:, 1024:2048], wvT[:, :])
            mil['wkvo'] = ld(e, qL, WARE[0:64, 2048:4096], woT[:, :])
        block.sync(_)

        xn_ld = [None] * 16
        for ec in range(16):
            def _(e, ec=ec):
                if ec == 0:
                    e.wait_ge(qL.h, sh_st[15])
                if ec >= 2:
                    e.wait_ge(cV.h, mil['xn_n%d' % (ec - 2)])
                xn_ld[ec] = ld(e, qL, (S0 if ec % 2 == 0 else S1)[:, :], xs_d.ap()[ec, :, :])
            block.sync(_)

            def _(e, ec=ec):
                e.wait_ge(qL.h, xn_ld[ec])
                if ec == 0:
                    e.wait_ge(cT.h, mil['l2_bc'])
                bx = S0 if ec % 2 == 0 else S1
                e.tensor_tensor(S2[:, :], bx[:, :], PS1[:, :], ALU.subtract)
                e.tensor_tensor(S2[:, :], S2[:, :], PS2[:, :], ALU.mult)
                cV.inc(e.tensor_scalar(AR2[:, ec * T:(ec + 1) * T], S2[:, :],
                                       slg[:, ec:ec + 1], slb[:, ec:ec + 1], ALU.mult, ALU.add))
                mil['xn_n%d' % ec] = cV.v
            block.vector(_)

            def _(e, ec=ec):
                e.wait_ge(cV.h, mil['xn_n%d' % ec])
                if ec == 0:
                    e.wait_ge(qL.h, mil['wkvo'])
                    e.wait_ge(cV.h, mil['l2_rc'])
                last = None
                for tb in range(2):
                    e.matmul(PS3[0:64, tb * 512:(tb + 1) * 512],
                             WARE[:, ec * 64:(ec + 1) * 64],
                             AR2[:, ec * T + tb * 512: ec * T + (tb + 1) * 512],
                             start=(ec == 0), stop=(ec == 15))
                    last = e.matmul(PS4[0:64, tb * 512:(tb + 1) * 512],
                                    WARE[:, 1024 + ec * 64: 1024 + (ec + 1) * 64],
                                    AR2[:, ec * T + tb * 512: ec * T + (tb + 1) * 512],
                                    start=(ec == 0), stop=(ec == 15))
                mil['kv%d' % ec] = cT.inc(last)
            block.tensor(_)

        if 'xnorm' in taps:
            def _(e):
                e.wait_ge(cT.h, mil['kv15'])
                for dcc in range(DCH):
                    ld(e, qA, taps['xnorm'].ap()[dcc, :, :], AR2[:, dcc * T:(dcc + 1) * T])
                mil['tap_xnorm'] = qA.v
            block.scalar(_)

        def _(e):
            e.wait_ge(cT.h, mil['kv15'])
            cS.inc(e.activation(S1.ap()[0:64, :], PS3[0:64, :], AF.Sigmoid))
            mil['sigk'] = cS.v
        block.scalar(_)

        def _(e):
            e.wait_ge(cS.h, mil['uw'])
            e.wait_ge(cV.h, mil['xn_n15'])
            e.memset(S0.ap()[0:64, :], 1.0)
            e.tensor_scalar_mul(S0.ap()[0:64, :], S0.ap()[0:64, :], sw1[:, 0:1])
            e.wait_ge(cS.h, mil['sigk'])
            cV.inc(e.tensor_scalar_mul(S1.ap()[0:64, :], S1.ap()[0:64, :], su[:, 0:1]))
            mil['m_k'] = cV.v
        block.vector(_)

        def _(e):
            e.wait_ge(cV.h, mil['m_k'])
            cS.inc(e.activation(S1.ap()[0:64, :], S1.ap()[0:64, :], AF.Exp, scale=-1.0))
            mil['e_k'] = cS.v
        block.scalar(_)

        def _(e):
            e.wait_ge(cS.h, mil['e_k'])
            e.tensor_tensor(S2.ap()[0:64, :], S1.ap()[0:64, :], PS4[0:64, :], ALU.mult)
            cV.inc(e.tensor_tensor_scan(S3.ap()[0:64, 0:512], S0.ap()[0:64, 0:512],
                                        S2.ap()[0:64, 0:512], szero[:, 0:1],
                                        ALU.mult, ALU.add))
            mil['scan1a'] = cV.v
        block.vector(_)

        def _(e):
            e.wait_ge(cV.h, mil['scan1a'])
            e.tensor_copy(sstash[:, :], S3.ap()[0:64, 511:512])
            e.tensor_copy(S4.ap()[0:64, 0:512], S2.ap()[0:64, 512:1024])
            for _j in range(10):
                e.tensor_copy(sP[:, 0:1], sones8[:, :])
            e.tensor_tensor_scan(S1.ap()[0:64, 0:512], S0.ap()[0:64, 0:512],
                                 S4.ap()[0:64, 0:512], sstash[:, 0:1],
                                 ALU.mult, ALU.add)
            cV.inc(e.tensor_copy(S3.ap()[0:64, 512:1024], S1.ap()[0:64, 0:512]))
            mil['scan1'] = cV.v
        block.vector(_)

        def _(e):
            e.wait_ge(cV.h, mil['scan1'])
            mil['seg_wr'] = ld(e, qA, seg_d.ap(), S3.ap()[0:64, 1023:1024])
        block.scalar(_)

        def _(e):
            e.wait_ge(qA.h, mil['seg_wr'])
            mil['ag_seg'] = cc.inc(e.collective_compute(
                "AllGather", ALU.bypass, replica_groups=[list(range(NCORE))],
                ins=[seg_d.ap()], outs=[segall.ap()]))
        block.gpsimd(_)

        def _(e):
            e.wait_ge(cc.h, mil['ag_seg'])
            mil['seg_ld'] = ld(e, qL, sseg[:, :], segall[:, :])
        block.sync(_)

        def _(e):
            e.wait_ge(cV.h, mil['xn_n15'])
            mil['E_mm'] = cT.inc(e.matmul(PS1[0:8, 0:64], sexpo[:, :], stdr[:, :],
                                          start=True, stop=True))
        block.tensor(_)

        def _(e):
            e.wait_ge(cT.h, mil['E_mm'])
            cS.inc(e.activation(sE[:, :], PS1[0:8, 0:64], AF.Exp))
            mil['E_exp'] = cS.v
        block.scalar(_)

        def _(e):
            e.wait_ge(cS.h, mil['E_exp'])
            e.wait_ge(qL.h, mil['seg_ld'])
            cV.inc(e.tensor_tensor(sP[:, :], sE[:, :], sseg[:, :], ALU.mult))
            mil['P'] = cV.v
        block.vector(_)

        def _(e):
            e.wait_ge(cV.h, mil['P'])
            mil['G_mm'] = cT.inc(e.matmul(PS1[0:64, 64:65], sP[:, :], sones8[:, :],
                                          start=True, stop=True))
        block.tensor(_)

        def _(e):
            e.wait_ge(cV.h, mil['scan1'])
            cG.inc(e.iota(S1.ap()[0:64, :], pattern=[[1, T]], base=1,
                          channel_multiplier=0, allow_small_or_imprecise_dtypes=True))
            mil['iota'] = cG.v
        block.gpsimd(_)

        def _(e):
            e.wait_ge(cG.h, mil['iota'])
            cV.inc(e.tensor_scalar_mul(S1.ap()[0:64, :], S1.ap()[0:64, :], stdc[:, 0:1]))
            mil['wpow_pre'] = cV.v
        block.vector(_)

        def _(e):
            e.wait_ge(cV.h, mil['wpow_pre'])
            cS.inc(e.activation(S1.ap()[0:64, :], S1.ap()[0:64, :], AF.Exp))
            mil['wpow'] = cS.v
        block.scalar(_)

        def _(e):
            e.wait_ge(cT.h, mil['G_mm'])
            e.wait_ge(cS.h, mil['wpow'])
            e.tensor_copy(sGv[:, :], PS1[0:64, 64:65])
            for _j in range(10):
                e.tensor_copy(sP[:, 0:1], sones8[:, :])
            e.tensor_scalar_mul(S1.ap()[0:64, :], S1.ap()[0:64, :], sGv[:, 0:1])
            cV.inc(e.tensor_tensor(S3.ap()[0:64, :], S3.ap()[0:64, :],
                                   S1.ap()[0:64, :], ALU.add))
            mil['scan2'] = cV.v
        block.vector(_)

        if 'states' in taps:
            def _(e):
                e.wait_ge(cV.h, mil['scan2'])
                ld(e, qA, taps['states'].ap(), S3.ap()[0:64, :])
                mil['tap_states'] = qA.v
                if 'sseg' in taps:
                    ld(e, qA, taps['sseg'].ap(), sseg[:, :])
                if 'sE' in taps:
                    ld(e, qA, taps['sE'].ap(), sE[:, :])
                if 'sP' in taps:
                    ld(e, qA, taps['sP'].ap(), sP[:, :])
                if 'sGv' in taps:
                    ld(e, qA, taps['sGv'].ap(), sGv[:, :])
            block.scalar(_)
        if 'wkv' in taps:
            def _(e):
                e.wait_ge(cV.h, mil['scan1'])
                ld(e, qA, taps['wkv'].ap(), S2.ap()[0:64, :])
            block.scalar(_)

        op_ld = [None] * 16
        op_mm = [None] * 16
        op_ev = [None] * 16
        op_st = [None] * 16
        for dc in range(16):
            def _(e, dc=dc):
                if dc < 2:
                    e.wait_ge(cV.h, mil['scan2'])
                else:
                    e.wait_ge(cV.h, op_ev[dc - 2])
                op_ld[dc] = ld(e, qL, (S0 if dc % 2 == 0 else S1)[:, :], xT.ap()[dc, :, :])
            block.sync(_)

            def _(e, dc=dc):
                if dc == 0:
                    e.wait_ge(cV.h, mil['scan2'])
                if dc >= 2:
                    e.wait_ge(cV.h, op_ev[dc - 2])
                ps = PS1 if dc % 2 == 0 else PS2
                last = None
                for tb in range(2):
                    last = e.matmul(ps[:, tb * 512:(tb + 1) * 512],
                                    WARE[0:64, 2048 + dc * 128: 2048 + (dc + 1) * 128],
                                    S3.ap()[0:64, tb * 512:(tb + 1) * 512],
                                    start=True, stop=True)
                op_mm[dc] = cT.inc(last)
            block.tensor(_)

            def _(e, dc=dc):
                e.wait_ge(cT.h, op_mm[dc])
                e.wait_ge(qL.h, op_ld[dc])
                if dc >= 2:
                    e.wait_ge(qA.h, op_st[dc - 2])
                ps = PS1 if dc % 2 == 0 else PS2
                bx = S0 if dc % 2 == 0 else S1
                bo = S2 if dc % 2 == 0 else S4
                op_ev[dc] = cV.inc(e.tensor_tensor(bo[:, :], ps[:, :], bx[:, :], ALU.add))
            block.vector(_)

            def _(e, dc=dc):
                e.wait_ge(cV.h, op_ev[dc])
                bo = S2 if dc % 2 == 0 else S4
                op_st[dc] = ld(e, qA, oa_d.ap()[dc, :, :], bo[:, :])
            block.scalar(_)

        fp_done = [None] * DCH
        for dc in range(DCH):
            def _(e, dc=dc):
                if dc == 0:
                    e.wait_ge(cV.h, op_ev[15])
                    e.wait_ge(qA.h, op_st[15])
                ch = AR1.ap()[:, dc * T:(dc + 1) * T]
                hi_b = S0b[:, 0:1024]
                lo_b = S1b[:, 0:1024]
                e.tensor_copy(hi_b, ch)
                e.tensor_tensor(lo_b, ch, hi_b, ALU.subtract)
                e.tensor_copy(AR1b[:, dc * 2048: dc * 2048 + 1024], hi_b)
                fp_done[dc] = cV.inc(
                    e.tensor_copy(AR1b[:, dc * 2048 + 1024: dc * 2048 + 2048], lo_b))
            block.vector(_)

        gu_view = wqgu_all.ap()
        pj_view = wqpj_all.ap()
        gubuf = [WARb[:, 0:4096], WARb[:, 4096:8192]]
        pjbuf = [WARb[:, 8192:10240], WARb[:, 10240:12288]]

        gu_ld = [None] * (NPAIR + 1)
        gu_mm = [None] * NPAIR
        gu_silu = [None] * NPAIR
        gu_h = [None] * NPAIR
        gu_split = [None] * NPAIR
        gu_sq = [None] * NPAIR
        gu_acc = [None] * NPAIR
        pj_ld = [[None] * 4 for _ in range(NGRP)]
        pj_mm = [[None] * 4 for _ in range(NGRP)]
        pj_ev = [[None] * 4 for _ in range(NGRP)]

        def gu_w_src(chunk):
            return gu_view[chunk // 16, chunk % 16, :, :, :]

        def _(e):
            e.wait_ge(cc.h, mil['ag_gu'])
            e.wait_ge(cT.h, op_mm[15])
            e.dma_start(out=gubuf[0][:, 0:2048].rearrange("p (a f) -> p a f", f=128),
                        in_=gu_w_src(0)).then_inc(qL.h, 16)
            qL.v += 16
            e.dma_start(out=gubuf[0][:, 2048:4096].rearrange("p (a f) -> p a f", f=128),
                        in_=gu_w_src(64)).then_inc(qL.h, 16)
            qL.v += 16
            gu_ld[0] = qL.v
        block.sync(_)

        ps_gate = [PS1, PS3]
        ps_up = [PS2, PS4]

        for g in range(NGRP):
            for pi in range(GPP):
                p = g * GPP + pi

                def _(e, p=p):
                    if p + 1 < NPAIR:
                        if p >= 1:
                            e.wait_ge(cT.h, gu_mm[p - 1])
                        e.dma_start(
                            out=gubuf[(p + 1) % 2][:, 0:2048].rearrange("p (a f) -> p a f", f=128),
                            in_=gu_w_src(p + 1)).then_inc(qL.h, 16)
                        qL.v += 16
                        e.dma_start(
                            out=gubuf[(p + 1) % 2][:, 2048:4096].rearrange("p (a f) -> p a f", f=128),
                            in_=gu_w_src(64 + p + 1)).then_inc(qL.h, 16)
                        qL.v += 16
                        gu_ld[p + 1] = qL.v
                block.sync(_)

                def _(e, p=p, g=g, pi=pi):
                    e.wait_ge(qL.h, gu_ld[p])
                    if p == 0:
                        e.wait_ge(cV.h, fp_done[DCH - 1])
                    if p >= 2:
                        e.wait_ge(cS.h, gu_silu[p - 2])
                        e.wait_ge(cV.h, gu_h[p - 2])
                    if g >= 1 and pi <= 1:
                        e.wait_ge(cV.h, pj_ev[g - 1][3])
                    wb = gubuf[p % 2]
                    pg = ps_gate[p % 2]
                    pu = ps_up[p % 2]
                    last = None
                    for tb in range(2):
                        n = 0
                        for dc in range(DCH):
                            for part in range(2):
                                rhs = AR1b[:, dc * 2048 + part * 1024 + tb * 512:
                                           dc * 2048 + part * 1024 + (tb + 1) * 512]
                                e.matmul(pg[:, tb * 512:(tb + 1) * 512],
                                         wb[:, dc * 128:(dc + 1) * 128], rhs,
                                         start=(n == 0), stop=(n == 31))
                                last = e.matmul(pu[:, tb * 512:(tb + 1) * 512],
                                                wb[:, 2048 + dc * 128: 2048 + (dc + 1) * 128],
                                                rhs, start=(n == 0), stop=(n == 31))
                                n += 1
                    gu_mm[p] = cT.inc(last)
                block.tensor(_)

                def _(e, p=p):
                    e.wait_ge(cT.h, gu_mm[p])
                    if p >= 1:
                        e.wait_ge(cV.h, gu_h[p - 1])
                    gu_silu[p] = cS.inc(e.activation(S0[:, :], ps_gate[p % 2][:, :], AF.Silu))
                block.scalar(_)

                def _(e, p=p, pi=pi):
                    e.wait_ge(cS.h, gu_silu[p])
                    if p >= 1:
                        e.wait_ge(cS.h, gu_sq[p - 1])
                    gu_h[p] = cV.inc(e.tensor_tensor(S1[:, :], S0[:, :],
                                                     ps_up[p % 2][:, :], ALU.mult))
                    hi_b = AR3b[:, pi * 2048: pi * 2048 + 1024]
                    lo_b = AR3b[:, pi * 2048 + 1024: pi * 2048 + 2048]
                    e.tensor_copy(hi_b, S1[:, :])
                    gu_split[p] = cV.inc(e.tensor_tensor(lo_b, S1[:, :], hi_b, ALU.subtract))
                block.vector(_)

                def _(e, p=p):
                    e.wait_ge(cV.h, gu_split[p])
                    if p >= 1:
                        e.wait_ge(cG.h, gu_acc[p - 1])
                    gu_sq[p] = cS.inc(e.square(S2[:, :], S1[:, :]))
                block.scalar(_)

                def _(e, p=p):
                    e.wait_ge(cS.h, gu_sq[p])
                    if p == 0:
                        if 'tap_states' in mil:
                            e.wait_ge(qA.h, mil['tap_states'])
                        gu_acc[p] = cG.inc(e.tensor_copy(S3[:, :], S2[:, :]))
                    else:
                        gu_acc[p] = cG.inc(e.tensor_tensor(S3[:, :], S3[:, :], S2[:, :], ALU.add))
                block.gpsimd(_)

            for dsub in range(4):
                def _(e, g=g, dsub=dsub):
                    if g == 0 and dsub == 0:
                        e.wait_ge(cc.h, mil['ag_pj'])
                    if dsub >= 2:
                        e.wait_ge(cT.h, pj_mm[g][dsub - 2])
                    elif g >= 1:
                        e.wait_ge(cT.h, pj_mm[g - 1][2 + dsub])
                    for k in range(GPP):
                        hc = g * GPP + k
                        src = pj_view[hc // 8, hc % 8, :, dsub * 4:(dsub + 1) * 4, :]
                        dst = pjbuf[dsub % 2][:, k * 512:(k + 1) * 512].rearrange(
                            "p (a f) -> p a f", f=128)
                        e.dma_start(out=dst, in_=src).then_inc(qL.h, 16)
                        qL.v += 16
                    pj_ld[g][dsub] = qL.v
                block.sync(_)

                def _(e, g=g, dsub=dsub):
                    e.wait_ge(qL.h, pj_ld[g][dsub])
                    if dsub == 0:
                        e.wait_ge(cS.h, gu_silu[g * GPP + GPP - 1])
                        e.wait_ge(cV.h, gu_h[g * GPP + GPP - 1])
                        e.wait_ge(cV.h, gu_split[g * GPP + GPP - 1])
                    else:
                        e.wait_ge(cV.h, pj_ev[g][dsub - 1])
                    PSL = [PS1, PS2, PS3, PS4]
                    last = None
                    for j in range(4):
                        for tb in range(2):
                            n = 0
                            for k in range(GPP):
                                for part in range(2):
                                    rhs = AR3b[:, k * 2048 + part * 1024 + tb * 512:
                                               k * 2048 + part * 1024 + (tb + 1) * 512]
                                    last = e.matmul(
                                        PSL[j][:, tb * 512:(tb + 1) * 512],
                                        pjbuf[dsub % 2][:, (k * 4 + j) * 128:(k * 4 + j + 1) * 128],
                                        rhs, start=(n == 0), stop=(n == 7))
                                    n += 1
                    pj_mm[g][dsub] = cT.inc(last)
                block.tensor(_)

                def _(e, g=g, dsub=dsub):
                    e.wait_ge(cT.h, pj_mm[g][dsub])
                    PSL = [PS1, PS2, PS3, PS4]
                    last = None
                    for j in range(4):
                        dc = dsub * 4 + j
                        dst = AR2.ap()[:, dc * T:(dc + 1) * T]
                        if g == 0:
                            last = e.tensor_copy(dst, PSL[j][:, :])
                        else:
                            last = e.tensor_tensor(dst, dst, PSL[j][:, :], ALU.add)
                    pj_ev[g][dsub] = cV.inc(last)
                block.vector(_)

        def _(e):
            e.wait_ge(cG.h, gu_acc[NPAIR - 1])
            e.wait_ge(cV.h, pj_ev[NGRP - 1][3])
            e.matmul(PS1[0:1, 0:512], sones128[:, :], S3[:, 0:512], start=True, stop=True)
            mil['rm_sums'] = cT.inc(e.matmul(
                PS1[0:1, 512:1024], sones128[:, :], S3[:, 512:1024], start=True, stop=True))
        block.tensor(_)

        emit_rows_finalize(PS1[0:1, 0:1024], None, 'rm', eps=RMS_EPS, is_rms=True,
                           nmean=8192, wait_t=lambda: mil['rm_sums'])

        def _(e):
            e.wait_ge(cV.h, mil['rm_rows'])
            e.matmul(PS2[:, 0:512], sonesrow[:, :], R_rs[:, 0:512], start=True, stop=True)
            mil['rm_bc'] = cT.inc(e.matmul(
                PS2[:, 512:1024], sonesrow[:, :], R_rs[:, 512:1024], start=True, stop=True))
        block.tensor(_)

        def _(e):
            e.wait_ge(cT.h, mil['rm_bc'])
            cV.inc(e.tensor_copy(S4[:, :], PS2[:, :]))
            mil['rsqb'] = cV.v
        block.vector(_)

        if 'F' in taps:
            def _(e):
                e.wait_ge(cV.h, pj_ev[NGRP - 1][3])
                for dcc in range(DCH):
                    ld(e, qA, taps['F'].ap()[dcc, :, :], AR2[:, dcc * T:(dcc + 1) * T])
            block.scalar(_)
        if 'xs' in taps:
            def _(e):
                e.wait_ge(cV.h, pj_ev[NGRP - 1][3])
                for dcc in range(DCH):
                    ld(e, qA, taps['xs'].ap()[dcc, :, :], xs_d.ap()[dcc, :, :])
            block.scalar(_)

        fo_ld = [None] * DCH
        fo_ev = [None] * DCH
        fo_st = [None] * DCH
        for dc in range(DCH):
            def _(e, dc=dc):
                if dc == 0:
                    e.wait_ge(cV.h, mil['rsqb'])
                if dc >= 2:
                    e.wait_ge(cV.h, fo_ev[dc - 2])
                fo_ld[dc] = ld(e, qL, (S0 if dc % 2 == 0 else S1)[:, :], oa_d.ap()[dc, :, :])
            block.sync(_)

            def _(e, dc=dc):
                e.wait_ge(qL.h, fo_ld[dc])
                if dc == 0:
                    e.wait_ge(cV.h, mil['rsqb'])
                if dc >= 2:
                    e.wait_ge(qA.h, fo_st[dc - 2])
                bx = S0 if dc % 2 == 0 else S1
                bo = AR3.ap()[:, (dc % 2) * 1024:(dc % 2 + 1) * 1024]
                e.tensor_tensor(S2[:, :], AR2[:, dc * T:(dc + 1) * T], S4[:, :], ALU.mult)
                fo_ev[dc] = cV.inc(e.tensor_tensor(bo, S2[:, :], bx[:, :], ALU.add))
            block.vector(_)

            def _(e, dc=dc):
                e.wait_ge(cV.h, fo_ev[dc])
                bo = AR3.ap()[:, (dc % 2) * 1024:(dc % 2 + 1) * 1024]
                fo_st[dc] = ld(e, qA, out_d.ap()[dc, :, :], bo)
            block.scalar(_)

        def _(e):
            e.wait_ge(qA.h, qA.v)
        block.scalar(_)

    return nc


def _stage_shared(W_shift, W_key, W_value, W_output, gate, ln_gamma, ln_beta,
                  time_decay, time_first):
    st = {}
    Ws = np.asarray(W_shift, np.float32)
    st['wsT'] = np.ascontiguousarray(Ws.reshape(16, 128, 16, 128).transpose(0, 3, 2, 1)).reshape(16, 128, 2048)
    Wk = np.asarray(W_key, np.float32)
    Wv = np.asarray(W_value, np.float32)
    st['wkT'] = np.ascontiguousarray(Wk.reshape(64, 16, 128).transpose(2, 1, 0)).reshape(128, 1024)
    st['wvT'] = np.ascontiguousarray(Wv.reshape(64, 16, 128).transpose(2, 1, 0)).reshape(128, 1024)
    Wo = np.asarray(W_output, np.float32)
    st['woT'] = np.ascontiguousarray(Wo.reshape(16, 128, 64).transpose(2, 0, 1)).reshape(64, 2048)
    st['gcols'] = np.ascontiguousarray(np.asarray(gate, np.float32).reshape(16, 128).T)
    st['lgcols'] = np.ascontiguousarray(np.asarray(ln_gamma, np.float32).reshape(16, 128).T)
    st['lbcols'] = np.ascontiguousarray(np.asarray(ln_beta, np.float32).reshape(16, 128).T)
    td = np.asarray(time_decay, np.float32)
    tf = np.asarray(time_first, np.float32)
    st['tdc'] = np.ascontiguousarray(td.reshape(64, 1))
    st['tdr'] = np.ascontiguousarray(td.reshape(1, 64))
    st['tfc'] = np.ascontiguousarray(tf.reshape(64, 1))
    ind = np.zeros((128, 2), np.float32)
    ind[0:64, 0] = 1.0
    ind[64:128, 1] = 1.0
    st['indst'] = ind.astype(bf)
    st['indbf'] = np.ascontiguousarray(ind.T)
    st['indbb'] = np.ascontiguousarray(ind.T).astype(bf)
    return st


def kernel(L0_attn_x, L0_attn_W_key, L0_attn_W_value, L0_attn_W_output,
           L0_attn_W_shift, L0_attn_gate, L0_attn_time_decay, L0_attn_time_first,
           L0_attn_ln_gamma, L0_attn_ln_beta, L0_ffn_w_gate_up, L0_ffn_w_proj):
    x = np.asarray(L0_attn_x, np.float32)[0]
    wgu = np.asarray(L0_ffn_w_gate_up, np.float32)
    wpj = np.asarray(L0_ffn_w_proj, np.float32)

    shared = _stage_shared(L0_attn_W_shift, L0_attn_W_key, L0_attn_W_value,
                           L0_attn_W_output, L0_attn_gate, L0_attn_ln_gamma,
                           L0_attn_ln_beta, L0_attn_time_decay, L0_attn_time_first)

    in_maps = []
    for c in range(NCORE):
        m = dict(shared)
        m['xT'] = np.ascontiguousarray(x[c * T:(c + 1) * T, :].T).reshape(DCH, 128, T)
        r0 = (c * T + 4096) % 8192
        m['xrT'] = np.ascontiguousarray(x[r0:r0 + T, :].T).reshape(DCH, 128, T)
        ex = np.zeros((1, 8), np.float32)
        for j in range(8):
            ex[0, j] = 1024.0 * (c - 1 - j) if j < c else 40000.0
        m['expo'] = ex
        m['qgu'] = np.ascontiguousarray(wgu[c * 2048:(c + 1) * 2048, :].T).reshape(16, 128, 2048)
        m['qpj'] = np.ascontiguousarray(wpj[:, c * 1024:(c + 1) * 1024].T).reshape(8, 128, 2048)
        in_maps.append(m)

    nc = build_program()
    res = run_bass_kernel_spmd(nc, in_maps, core_ids=list(range(NCORE)))

    out = np.empty((1, 8192, 2048), np.float32)
    for c in range(NCORE):
        o = np.asarray(res.results[c]["out"])
        out[0, c * T:(c + 1) * T, :] = o.reshape(2048, T).T
    if _DEBUG_TAPS:
        kernel._last_results = res
    return out


# version 06 (final)
# speedup vs baseline: 1.2141x; 1.2141x over previous
import numpy as np
import ml_dtypes
from contextlib import ExitStack

import concourse.bass as bass
import concourse.mybir as mybir
from concourse.bass_utils import run_bass_kernel_spmd

F32 = mybir.dt.float32
BF16 = mybir.dt.bfloat16
AF = mybir.ActivationFunctionType
ALU = mybir.AluOpType

NCORE = 8
T = 1024
D = 2048
DCH = 16
NPAIR = 64
GPP = 4
NGRP = NPAIR // GPP
LN_EPS = 1e-5
RMS_EPS = float(np.finfo(np.float32).eps)

bf = ml_dtypes.bfloat16

_DEBUG_TAPS = []


class Ctr:
    def __init__(self, h):
        self.h = h
        self.v = 0

    def inc(self, inst, n=1):
        inst.then_inc(self.h, n)
        self.v += n
        return self.v


def build_program():
    nc = bass.Bass("TRN2", target_bir_lowering=False, detect_race_conditions=False)

    xT = nc.dram_tensor("xT", [DCH, 128, T], F32, kind="ExternalInput")
    xrT = nc.dram_tensor("xrT", [DCH, 128, T], F32, kind="ExternalInput")
    wsT = nc.dram_tensor("wsT", [16, 128, 2048], F32, kind="ExternalInput")
    wkT = nc.dram_tensor("wkT", [128, 1024], F32, kind="ExternalInput")
    wvT = nc.dram_tensor("wvT", [128, 1024], F32, kind="ExternalInput")
    woT = nc.dram_tensor("woT", [64, 2048], F32, kind="ExternalInput")
    gcols = nc.dram_tensor("gcols", [128, 16], F32, kind="ExternalInput")
    lgcols = nc.dram_tensor("lgcols", [128, 16], F32, kind="ExternalInput")
    lbcols = nc.dram_tensor("lbcols", [128, 16], F32, kind="ExternalInput")
    tdc = nc.dram_tensor("tdc", [64, 1], F32, kind="ExternalInput")
    tdr = nc.dram_tensor("tdr", [1, 64], F32, kind="ExternalInput")
    tfc = nc.dram_tensor("tfc", [64, 1], F32, kind="ExternalInput")
    expo = nc.dram_tensor("expo", [1, 8], F32, kind="ExternalInput")
    indst = nc.dram_tensor("indst", [128, 2], BF16, kind="ExternalInput")
    indbf = nc.dram_tensor("indbf", [2, 128], F32, kind="ExternalInput")
    indbb = nc.dram_tensor("indbb", [2, 128], BF16, kind="ExternalInput")
    qgu = nc.dram_tensor("qgu", [16, 128, 2048], F32, kind="ExternalInput")
    qpj = nc.dram_tensor("qpj", [8, 128, 2048], F32, kind="ExternalInput")

    out_d = nc.dram_tensor("out", [DCH, 128, T], F32, kind="ExternalOutput")
    taps = {}
    for name in _DEBUG_TAPS:
        shp = {"xn": [DCH, 128, T], "xnorm": [DCH, 128, T], "wkv": [64, T],
               "states": [64, T], "F": [DCH, 128, T],
               "sseg": [8, 64], "sE": [8, 64], "sP": [8, 64], "sGv": [64, 1],
               "sa": [64, T], "stash": [64, 1], "wkv2": [64, T],
               "xs": [DCH, 128, T], "xnorm": [DCH, 128, T]}[name]
        taps[name] = nc.dram_tensor("tap_" + name, shp, F32, kind="ExternalOutput")

    xs_d = nc.dram_tensor("xs_d", [DCH, 128, T], F32)
    oa_d = nc.dram_tensor("oa_d", [DCH, 128, T], F32)
    seg_d = nc.dram_tensor("seg_d", [64, 1], F32)
    segall = nc.dram_tensor("segall", [8, 64], F32)
    wqgu_sh = nc.dram_tensor("wqgu_sh", [16, 128, 16, 128], BF16)
    wqgu_all = nc.dram_tensor("wqgu_all", [8, 16, 128, 16, 128], BF16)
    wqpj_sh = nc.dram_tensor("wqpj_sh", [8, 128, 16, 128], BF16)
    wqpj_all = nc.dram_tensor("wqpj_all", [8, 8, 128, 16, 128], BF16)

    with ExitStack() as st_:
        def sbuf(name, shape, dtype):
            return st_.enter_context(nc.sbuf_tensor(name, shape, dtype))

        AR1 = sbuf("AR1", [128, DCH * T], F32)
        AR2 = sbuf("AR2", [128, DCH * T], F32)
        AR3 = sbuf("AR3", [128, 4096], F32)
        WARE = sbuf("WARE", [128, 8704], F32)
        S0 = sbuf("S0", [128, T], F32)
        S1 = sbuf("S1", [128, T], F32)
        S2 = sbuf("S2", [128, T], F32)
        S3 = sbuf("S3", [128, T], F32)
        S4 = sbuf("S4", [128, T], F32)
        sgc = sbuf("sgc", [128, 16], F32)
        somg = sbuf("somg", [128, 16], F32)
        slg = sbuf("slg", [128, 16], F32)
        slb = sbuf("slb", [128, 16], F32)
        stdc = sbuf("stdc", [64, 1], F32)
        stdr = sbuf("stdr", [1, 64], F32)
        stfc = sbuf("stfc", [64, 1], F32)
        sexpo = sbuf("sexpo", [1, 8], F32)
        sindst = sbuf("sindst", [128, 2], BF16)
        sindbf = sbuf("sindbf", [2, 128], F32)
        sindbb = sbuf("sindbb", [2, 128], BF16)
        su = sbuf("su", [64, 1], F32)
        sw1 = sbuf("sw1", [64, 1], F32)
        sseg = sbuf("sseg", [8, 64], F32)
        sE = sbuf("sE", [8, 64], F32)
        sP = sbuf("sP", [8, 64], F32)
        sGv = sbuf("sGv", [64, 1], F32)
        szero = sbuf("szero", [64, 1], F32)
        sstash = sbuf("sstash", [64, 1], F32)
        sones8 = sbuf("sones8", [8, 1], F32)
        sones128 = sbuf("sones128", [128, 1], F32)
        sonesrow = sbuf("sonesrow", [1, 128], F32)

        PS1 = st_.enter_context(nc.psum_tensor("PS1", [128, 1024], F32))
        PS2 = st_.enter_context(nc.psum_tensor("PS2", [128, 1024], F32))
        PS3 = st_.enter_context(nc.psum_tensor("PS3", [128, 1024], F32))
        PS4 = st_.enter_context(nc.psum_tensor("PS4", [128, 1024], F32))

        qL = Ctr(st_.enter_context(nc.semaphore("qL")))
        qA = Ctr(st_.enter_context(nc.semaphore("qA")))
        qG = Ctr(st_.enter_context(nc.semaphore("qG")))
        cT = Ctr(st_.enter_context(nc.semaphore("cT")))
        cV = Ctr(st_.enter_context(nc.semaphore("cV")))
        cS = Ctr(st_.enter_context(nc.semaphore("cS")))
        cG = Ctr(st_.enter_context(nc.semaphore("cG")))
        cc = Ctr(st_.enter_context(nc.semaphore("cc")))

        block = st_.enter_context(nc.Block())

        AR1b = AR1.ap().bitcast(BF16)
        AR3b = AR3.ap().bitcast(BF16)
        WARb = WARE.ap().bitcast(BF16)
        S0b = S0.ap().bitcast(BF16)
        S1b = S1.ap().bitcast(BF16)

        w_wbf = [WARb[:, 0:2048], WARb[:, 2048:4096]]
        w_wabs = WARb[:, 4096:6144]
        w_rq = WARb[:, 6144:7168]
        w_d1 = WARb[:, 7168:8192]
        w_d2 = WARb[:, 8192:9216]
        w_dd = WARb[:, 9216:10240]
        w_wqb = WARb[:, 10240:12288]
        a_scf = WARE.ap()[0:2, 6144:7168]
        a_rs = WARE.ap()[0:2, 7168:8192]
        sscbf = WARb[0:2, 16384:17408]
        WS0 = AR3.ap()[:, 0:2048]
        WS1 = AR3.ap()[:, 2048:4096]

        R_mu = S3.ap()[0:1, 0:1024]
        R_rs = S4.ap()[0:1, 0:1024]
        tmpB = AR3.ap()[0:1, 0:1024]

        def srow(t):
            return t.ap()[0:1, 0:1024]

        mil = {}

        def ld(e, q, out, in_):
            q.inc(e.dma_start(out=out, in_=in_), 16)
            return q.v

        def _(e):
            ld(e, qL, sgc[:, :], gcols[:, :])
            ld(e, qL, slg[:, :], lgcols[:, :])
            ld(e, qL, slb[:, :], lbcols[:, :])
            ld(e, qL, stdc[:, :], tdc[:, :])
            ld(e, qL, stdr[:, :], tdr[:, :])
            ld(e, qL, stfc[:, :], tfc[:, :])
            ld(e, qL, sexpo[:, :], expo[:, :])
            ld(e, qL, sindst[:, :], indst[:, :])
            ld(e, qL, sindbf[:, :], indbf[:, :])
            mil['smalls'] = ld(e, qL, sindbb[:, :], indbb[:, :])
        block.sync(_)

        def _(e):
            e.memset(szero[:, :], 0.0)
            e.memset(sones8[:, :], 1.0)
            e.memset(sones128[:, :], 1.0)
            cV.inc(e.memset(sonesrow[:, :], 1.0))
            mil['consts'] = cV.v
        block.vector(_)

        def _(e):
            e.wait_ge(qL.h, mil['smalls'])
            cS.inc(e.activation(sgc[:, :], sgc[:, :], AF.Sigmoid))
            mil['gsig'] = cS.v
        block.scalar(_)

        def _(e):
            e.wait_ge(cS.h, mil['gsig'])
            cV.inc(e.tensor_scalar(somg[:, :], sgc[:, :], -1.0, 1.0, ALU.mult, ALU.add))
            mil['gates'] = cV.v
        block.vector(_)

        def _(e):
            e.wait_ge(qL.h, mil['smalls'])
            e.activation(su[:, :], stfc[:, :], AF.Exp)
            cS.inc(e.activation(sw1[:, :], stdc[:, :], AF.Exp))
            mil['uw'] = cS.v
        block.scalar(_)

        def q_src(i):
            return qgu.ap()[i, :, :] if i < 16 else qpj.ap()[i - 16, :, :]

        def q_dst(i):
            if i < 16:
                return wqgu_sh.ap()[:, :, i, :].transpose([1, 0, 2])
            return wqpj_sh.ap()[i - 16, :, :, :]

        NQ = 24
        q_cast = [None] * (NQ + 1)
        q_abs = [None] * NQ
        q_stats = [[None, None] for _ in range(NQ)]
        q_scale = [[None, None] for _ in range(NQ)]
        q_bc = [[None, None] for _ in range(NQ)]
        q_r = [[None, None] for _ in range(NQ)]
        q_dd = [[None, None] for _ in range(NQ)]
        q_wq = [[None, None] for _ in range(NQ)]
        q_wr = [None] * NQ

        def _(e):
            q_cast[0] = ld(e, qG, w_wbf[0], q_src(0))
        block.gpsimd(_)

        a_scf2 = AR3.ap()[0:2, 0:1024]
        a_rs2 = AR3.ap()[0:2, 1024:2048]
        sscbf2 = AR3b[0:2, 4096:5120]
        w_wabs2 = AR3b[:, 5120:7168]

        def emit_qtile(i):
            buf = w_wbf[i % 2]
            PST = PS4 if i % 2 == 0 else PS2
            PSR = PS3 if i % 2 == 0 else PS1
            scf_ = a_scf if i % 2 == 0 else a_scf2
            rs_ = a_rs if i % 2 == 0 else a_rs2
            scb_ = sscbf if i % 2 == 0 else sscbf2
            wab_ = w_wabs if i % 2 == 0 else w_wabs2

            def _(e):
                if i + 1 < NQ:
                    if i >= 1:
                        e.wait_ge(cV.h, q_r[i - 1][1])
                    q_cast[i + 1] = ld(e, qG, w_wbf[(i + 1) % 2], q_src(i + 1))
            block.gpsimd(_)

            def _(e):
                e.wait_ge(qG.h, q_cast[i])
                if i >= 2:
                    e.wait_ge(cT.h, q_stats[i - 2][1])
                q_abs[i] = cS.inc(e.activation(wab_, buf, AF.Abs))
            block.scalar(_)

            for h in range(2):
                cols = slice(h * 1024, (h + 1) * 1024)

                def _(e, h=h):
                    if h == 0:
                        e.wait_ge(cS.h, q_abs[i])
                        if i == 1:
                            e.wait_ge(cV.h, mil['ln1r_n%d' % (DCH - 1)])
                        if i >= 2:
                            e.wait_ge(cV.h, q_wq[i - 2][1])
                    else:
                        e.wait_ge(cV.h, q_wq[i][0])
                    c0 = h * 1024
                    e.matmul(PST[0:2, 0:512], sindst[:, :], wab_[:, c0:c0 + 512],
                             start=True, stop=True)
                    q_stats[i][h] = cT.inc(e.matmul(
                        PST[0:2, 512:1024], sindst[:, :], wab_[:, c0 + 512:c0 + 1024],
                        start=True, stop=True))
                block.tensor(_)

                def _(e, h=h):
                    e.wait_ge(cT.h, q_stats[i][h])
                    if h == 1:
                        e.wait_ge(cT.h, q_bc[i][0])
                    elif i >= 2:
                        e.wait_ge(cT.h, q_bc[i - 2][1])
                    e.tensor_scalar(scf_, PST[0:2, 0:1024], 1.0 / 64, 1e-8, ALU.mult, ALU.max)
                    e.tensor_copy(scb_, scf_)
                    q_scale[i][h] = cV.inc(e.reciprocal(rs_, scb_))
                block.vector(_)

                def _(e, h=h):
                    e.wait_ge(cV.h, q_scale[i][h])
                    prev_r = q_r[i][0] if h == 1 else (q_r[i - 2][1] if i >= 2 else None)
                    if prev_r is not None:
                        e.wait_ge(cV.h, prev_r)
                    e.matmul(PSR[:, 0:512], sindbf[:, :], rs_[:, 0:512], start=True, stop=True)
                    e.matmul(PSR[:, 512:1024], sindbf[:, :], rs_[:, 512:1024], start=True, stop=True)
                    e.matmul(PST[:, 0:512], sindbb[:, :], scb_[:, 0:512],
                             start=True, stop=True)
                    q_bc[i][h] = cT.inc(e.matmul(
                        PST[:, 512:1024], sindbb[:, :], scb_[:, 512:1024],
                        start=True, stop=True))
                block.tensor(_)

                def _(e, h=h, cols=cols):
                    e.wait_ge(cT.h, q_bc[i][h])
                    q_r[i][h] = cV.inc(e.tensor_tensor(w_rq, buf[:, cols], PSR[:, :], ALU.mult))
                    e.tensor_single_scalar(w_d1, w_rq, 0.5, ALU.is_gt)
                    e.tensor_single_scalar(w_d2, w_rq, -0.5, ALU.is_lt)
                    e.tensor_tensor(w_dd, w_d1, w_d2, ALU.subtract)
                    if h == 0 and i >= 1:
                        e.wait_ge(qA.h, q_wr[i - 1])
                    q_wq[i][h] = cV.inc(e.tensor_tensor(
                        w_wqb[:, cols], w_dd, PST[:, :], ALU.mult))
                    q_dd[i][h] = q_wq[i][h]
                block.vector(_)

            def _(e):
                e.wait_ge(cV.h, q_wq[i][1])
                src = w_wqb.rearrange("p (a f) -> p a f", f=128)
                q_wr[i] = ld(e, qA, q_dst(i), src)
            block.scalar(_)

        l1_ld = [None] * DCH
        l1_sq = [None] * DCH
        l1_mm = [None] * DCH

        def emit_l1_stats(dc, wait_v_first=None):
            sq_buf = S3 if dc % 2 == 0 else S4

            def _(e):
                e.wait_ge(qL.h, l1_ld[dc])
                if dc >= 2:
                    e.wait_ge(cT.h, l1_mm[dc - 2])
                l1_sq[dc] = cS.inc(e.square(sq_buf[:, :], AR2[:, dc * T:(dc + 1) * T]))
            block.scalar(_)

            def _(e):
                e.wait_ge(cS.h, l1_sq[dc])
                if dc == 0 and wait_v_first is not None:
                    e.wait_ge(cV.h, wait_v_first)
                ch = AR2.ap()[:, dc * T:(dc + 1) * T]
                e.matmul(PS1[0:1, 0:512], sones128[:, :], ch[:, 0:512],
                         start=(dc == 0), stop=(dc == DCH - 1))
                e.matmul(PS1[0:1, 512:1024], sones128[:, :], ch[:, 512:1024],
                         start=(dc == 0), stop=(dc == DCH - 1))
                e.matmul(PS2[0:1, 0:512], sones128[:, :], sq_buf[:, 0:512],
                         start=(dc == 0), stop=(dc == DCH - 1))
                l1_mm[dc] = cT.inc(e.matmul(
                    PS2[0:1, 512:1024], sones128[:, :], sq_buf[:, 512:1024],
                    start=(dc == 0), stop=(dc == DCH - 1)))
            block.tensor(_)

        def emit_rows_finalize(sx_row, sx2_row, tag, eps=LN_EPS, is_rms=False, nmean=None,
                               wait_t=None):
            tmpA = srow(S2)

            def _(e):
                if wait_t is not None:
                    e.wait_ge(cT.h, wait_t())
                if is_rms:
                    e.tensor_scalar(R_mu, sx_row, 1.0 / nmean, eps, ALU.mult, ALU.add)
                    e.tensor_copy(R_rs, R_mu)
                else:
                    e.tensor_scalar_mul(R_mu, sx_row, 1.0 / nmean)
                    e.tensor_scalar_mul(R_rs, sx2_row, 1.0 / nmean)
                    e.tensor_tensor(tmpA, R_mu, R_mu, ALU.mult)
                    e.tensor_tensor(R_rs, R_rs, tmpA, ALU.subtract)
                    e.tensor_scalar_add(R_rs, R_rs, eps)
                cV.inc(e.reciprocal(tmpA, R_rs))
                mil[tag + '_rc'] = cV.v
            block.vector(_)

            def _(e):
                e.wait_ge(cV.h, mil[tag + '_rc'])
                cS.inc(e.sqrt(tmpA, tmpA))
                mil[tag + '_y0'] = cS.v
            block.scalar(_)

            def _(e):
                e.wait_ge(cS.h, mil[tag + '_y0'])
                e.tensor_tensor(tmpB, tmpA, tmpA, ALU.mult)
                e.tensor_tensor(tmpB, tmpB, R_rs, ALU.mult)
                e.tensor_scalar(tmpB, tmpB, -0.5, 1.5, ALU.mult, ALU.add)
                cV.inc(e.tensor_tensor(R_rs, tmpA, tmpB, ALU.mult))
                mil[tag + '_rows'] = cV.v
            block.vector(_)

        def emit_ln_finalize(eps, tag):
            emit_rows_finalize(PS1[0:1, 0:1024], PS2[0:1, 0:1024], tag, eps=eps, nmean=D,
                               wait_t=lambda: l1_mm[DCH - 1])

            def _(e):
                e.wait_ge(cV.h, mil[tag + '_rows'])
                e.matmul(PS1[:, 0:512], sonesrow[:, :], R_mu[:, 0:512], start=True, stop=True)
                e.matmul(PS1[:, 512:1024], sonesrow[:, :], R_mu[:, 512:1024], start=True, stop=True)
                e.matmul(PS2[:, 0:512], sonesrow[:, :], R_rs[:, 0:512], start=True, stop=True)
                mil[tag + '_bc'] = cT.inc(e.matmul(
                    PS2[:, 512:1024], sonesrow[:, :], R_rs[:, 512:1024], start=True, stop=True))
            block.tensor(_)

        def emit_xn_pass(dst, tag):
            for dc in range(DCH):
                def _(e, dc=dc):
                    if dc == 0:
                        e.wait_ge(cT.h, mil[tag + '_bc'])
                    ch_in = AR2.ap()[:, dc * T:(dc + 1) * T]
                    e.tensor_tensor(S2[:, :], ch_in, PS1[:, :], ALU.subtract)
                    cV.inc(e.tensor_tensor(dst[:, dc * T:(dc + 1) * T], S2[:, :], PS2[:, :], ALU.mult))
                    mil[tag + '_n%d' % dc] = cV.v
                block.vector(_)

        for dc in range(DCH):
            def _(e, dc=dc):
                l1_ld[dc] = ld(e, qL, AR2[:, dc * T:(dc + 1) * T], xT.ap()[dc, :, :])
            block.sync(_)
            emit_l1_stats(dc)
        emit_ln_finalize(LN_EPS, 'ln1x')
        emit_xn_pass(AR1.ap(), 'ln1x')

        for dc in range(DCH):
            def _(e, dc=dc):
                e.wait_ge(cV.h, mil['ln1x_n%d' % dc])
                l1_ld[dc] = ld(e, qL, AR2[:, dc * T:(dc + 1) * T], xrT.ap()[dc, :, :])
            block.sync(_)
            emit_l1_stats(dc, wait_v_first=mil['ln1x_n%d' % (DCH - 1)] if dc == 0 else None)
        emit_ln_finalize(LN_EPS, 'ln1r')
        emit_xn_pass(AR2.ap(), 'ln1r')

        qsched = {}
        for _qi in range(16):
            emit_qtile(_qi)

        def _(e):
            e.wait_ge(qA.h, q_wr[15])
            mil['ag_gu'] = cc.inc(e.collective_compute(
                "AllGather", ALU.bypass, replica_groups=[list(range(NCORE))],
                ins=[wqgu_sh.ap()], outs=[wqgu_all.ap()]))
        block.gpsimd(_)
        for _qi in range(16, 24):
            emit_qtile(_qi)

        def _(e):
            e.wait_ge(qA.h, q_wr[23])
            mil['ag_pj'] = cc.inc(e.collective_compute(
                "AllGather", ALU.bypass, replica_groups=[list(range(NCORE))],
                ins=[wqpj_sh.ap()], outs=[wqpj_all.ap()]))
        block.gpsimd(_)

        def _(e):
            e.wait_ge(cV.h, mil['ln1r_rows'])
            e.wait_ge(cc.h, 2)
            mil['ws0'] = ld(e, qL, WS0, wsT.ap()[0, :, :])
        block.sync(_)

        sh_mm = [None] * 16
        sh_xs = [None] * 16
        sh_ev = [None] * 16
        sh_sq = [None] * 16
        sh_x2 = [None] * 16
        sh_st = [None] * 16
        for ec in range(16):
            def _(e, ec=ec):
                if ec + 1 < 16:
                    if ec >= 1:
                        e.wait_ge(cT.h, sh_mm[ec - 1])
                    mil['ws%d' % (ec + 1)] = ld(
                        e, qL, (WS1 if (ec + 1) % 2 else WS0), wsT.ap()[ec + 1, :, :])
            block.sync(_)

            def _(e, ec=ec):
                e.wait_ge(qL.h, mil['ws%d' % ec])
                if ec == 0:
                    e.wait_ge(cV.h, mil['ln1r_n%d' % (DCH - 1)])
                if ec >= 2:
                    e.wait_ge(cV.h, sh_xs[ec - 2])
                ps = PS1 if ec % 2 == 0 else PS2
                wb = WS0 if ec % 2 == 0 else WS1
                last = None
                for tb in range(2):
                    for dc in range(DCH):
                        last = e.matmul(ps[:, tb * 512:(tb + 1) * 512],
                                        wb[:, dc * 128:(dc + 1) * 128],
                                        AR2[:, dc * T + tb * 512: dc * T + (tb + 1) * 512],
                                        start=(dc == 0), stop=(dc == DCH - 1))
                sh_mm[ec] = cT.inc(last)
            block.tensor(_)

            def _(e, ec=ec):
                e.wait_ge(cT.h, sh_mm[ec])
                if ec == 0:
                    e.wait_ge(cV.h, mil['gates'])
                if ec >= 2:
                    e.wait_ge(qL.h, sh_st[ec - 2])
                ps = PS1 if ec % 2 == 0 else PS2
                bxs = S0 if ec % 2 == 0 else S1
                e.tensor_scalar_mul(S2[:, :], AR1[:, ec * T:(ec + 1) * T], somg[:, ec:ec + 1])
                sh_xs[ec] = cV.inc(e.scalar_tensor_tensor(
                    bxs[:, :], ps[:, :], sgc[:, ec:ec + 1], S2[:, :], ALU.mult, ALU.add))
                if ec == 0:
                    sh_ev[ec] = cV.inc(e.tensor_copy(S3[:, :], bxs[:, :]))
                else:
                    sh_ev[ec] = cV.inc(e.tensor_tensor(S3[:, :], S3[:, :], bxs[:, :], ALU.add))
            block.vector(_)

            def _(e, ec=ec):
                e.wait_ge(cV.h, sh_xs[ec])
                bxs = S0 if ec % 2 == 0 else S1
                sh_sq[ec] = cS.inc(e.square(S2[:, :], bxs[:, :]))
            block.scalar(_)

            def _(e, ec=ec):
                e.wait_ge(cS.h, sh_sq[ec])
                if ec == 0:
                    sh_x2[ec] = cV.inc(e.tensor_copy(S4[:, :], S2[:, :]))
                else:
                    sh_x2[ec] = cV.inc(e.tensor_tensor(S4[:, :], S4[:, :], S2[:, :], ALU.add))
            block.vector(_)

            def _(e, ec=ec):
                e.wait_ge(cV.h, sh_xs[ec])
                bxs = S0 if ec % 2 == 0 else S1
                sh_st[ec] = ld(e, qL, xs_d.ap()[ec, :, :], bxs[:, :])
            block.sync(_)

            for qi in qsched.get(ec, []):
                emit_qtile(qi)

        def _(e):
            e.wait_ge(cV.h, q_wq[23][1])
            e.wait_ge(cV.h, sh_x2[15])
            e.matmul(PS3[0:1, 0:512], sones128[:, :], S3[:, 0:512], start=True, stop=True)
            e.matmul(PS3[0:1, 512:1024], sones128[:, :], S3[:, 512:1024], start=True, stop=True)
            e.matmul(PS4[0:1, 0:512], sones128[:, :], S4[:, 0:512], start=True, stop=True)
            mil['l2_sums'] = cT.inc(e.matmul(
                PS4[0:1, 512:1024], sones128[:, :], S4[:, 512:1024], start=True, stop=True))
        block.tensor(_)

        emit_rows_finalize(PS3[0:1, 0:1024], PS4[0:1, 0:1024], 'l2', eps=LN_EPS, nmean=D,
                           wait_t=lambda: mil['l2_sums'])

        def _(e):
            e.wait_ge(cV.h, mil['l2_rows'])
            e.wait_ge(cV.h, sh_xs[15])
            e.matmul(PS1[:, 0:512], sonesrow[:, :], R_mu[:, 0:512], start=True, stop=True)
            e.matmul(PS1[:, 512:1024], sonesrow[:, :], R_mu[:, 512:1024], start=True, stop=True)
            e.matmul(PS2[:, 0:512], sonesrow[:, :], R_rs[:, 0:512], start=True, stop=True)
            mil['l2_bc'] = cT.inc(e.matmul(
                PS2[:, 512:1024], sonesrow[:, :], R_rs[:, 512:1024], start=True, stop=True))
        block.tensor(_)

        def _(e):
            e.wait_ge(cV.h, q_wq[23][1])
            e.wait_ge(qA.h, q_wr[23])
            ld(e, qL, WARE[:, 0:1024], wkT[:, :])
            ld(e, qL, WARE[:, 1024:2048], wvT[:, :])
            mil['wkvo'] = ld(e, qL, WARE[0:64, 2048:4096], woT[:, :])
        block.sync(_)

        xn_ld = [None] * 16
        for ec in range(16):
            def _(e, ec=ec):
                if ec == 0:
                    e.wait_ge(qL.h, sh_st[15])
                if ec >= 2:
                    e.wait_ge(cV.h, mil['xn_n%d' % (ec - 2)])
                xn_ld[ec] = ld(e, qL, (S0 if ec % 2 == 0 else S1)[:, :], xs_d.ap()[ec, :, :])
            block.sync(_)

            def _(e, ec=ec):
                e.wait_ge(qL.h, xn_ld[ec])
                if ec == 0:
                    e.wait_ge(cT.h, mil['l2_bc'])
                bx = S0 if ec % 2 == 0 else S1
                e.tensor_tensor(S2[:, :], bx[:, :], PS1[:, :], ALU.subtract)
                e.tensor_tensor(S2[:, :], S2[:, :], PS2[:, :], ALU.mult)
                cV.inc(e.tensor_scalar(AR2[:, ec * T:(ec + 1) * T], S2[:, :],
                                       slg[:, ec:ec + 1], slb[:, ec:ec + 1], ALU.mult, ALU.add))
                mil['xn_n%d' % ec] = cV.v
            block.vector(_)

            def _(e, ec=ec):
                e.wait_ge(cV.h, mil['xn_n%d' % ec])
                if ec == 0:
                    e.wait_ge(qL.h, mil['wkvo'])
                    e.wait_ge(cV.h, mil['l2_rc'])
                last = None
                for tb in range(2):
                    e.matmul(PS3[0:64, tb * 512:(tb + 1) * 512],
                             WARE[:, ec * 64:(ec + 1) * 64],
                             AR2[:, ec * T + tb * 512: ec * T + (tb + 1) * 512],
                             start=(ec == 0), stop=(ec == 15))
                    last = e.matmul(PS4[0:64, tb * 512:(tb + 1) * 512],
                                    WARE[:, 1024 + ec * 64: 1024 + (ec + 1) * 64],
                                    AR2[:, ec * T + tb * 512: ec * T + (tb + 1) * 512],
                                    start=(ec == 0), stop=(ec == 15))
                mil['kv%d' % ec] = cT.inc(last)
            block.tensor(_)

        if 'xnorm' in taps:
            def _(e):
                e.wait_ge(cT.h, mil['kv15'])
                for dcc in range(DCH):
                    ld(e, qA, taps['xnorm'].ap()[dcc, :, :], AR2[:, dcc * T:(dcc + 1) * T])
                mil['tap_xnorm'] = qA.v
            block.scalar(_)

        def _(e):
            e.wait_ge(cT.h, mil['kv15'])
            cS.inc(e.activation(S1.ap()[0:64, :], PS3[0:64, :], AF.Sigmoid))
            mil['sigk'] = cS.v
        block.scalar(_)

        def _(e):
            e.wait_ge(cS.h, mil['uw'])
            e.wait_ge(cV.h, mil['xn_n15'])
            e.memset(S0.ap()[0:64, :], 1.0)
            e.tensor_scalar_mul(S0.ap()[0:64, :], S0.ap()[0:64, :], sw1[:, 0:1])
            e.wait_ge(cS.h, mil['sigk'])
            cV.inc(e.tensor_scalar_mul(S1.ap()[0:64, :], S1.ap()[0:64, :], su[:, 0:1]))
            mil['m_k'] = cV.v
        block.vector(_)

        def _(e):
            e.wait_ge(cV.h, mil['m_k'])
            cS.inc(e.activation(S1.ap()[0:64, :], S1.ap()[0:64, :], AF.Exp, scale=-1.0))
            mil['e_k'] = cS.v
        block.scalar(_)

        def _(e):
            e.wait_ge(cS.h, mil['e_k'])
            e.tensor_tensor(S2.ap()[0:64, :], S1.ap()[0:64, :], PS4[0:64, :], ALU.mult)
            cV.inc(e.tensor_tensor_scan(S3.ap()[0:64, 0:512], S0.ap()[0:64, 0:512],
                                        S2.ap()[0:64, 0:512], szero[:, 0:1],
                                        ALU.mult, ALU.add))
            mil['scan1a'] = cV.v
        block.vector(_)

        def _(e):
            e.wait_ge(cV.h, mil['scan1a'])
            e.tensor_copy(sstash[:, :], S3.ap()[0:64, 511:512])
            e.tensor_copy(S4.ap()[0:64, 0:512], S2.ap()[0:64, 512:1024])
            for _j in range(10):
                e.tensor_copy(sP[:, 0:1], sones8[:, :])
            e.tensor_tensor_scan(S1.ap()[0:64, 0:512], S0.ap()[0:64, 0:512],
                                 S4.ap()[0:64, 0:512], sstash[:, 0:1],
                                 ALU.mult, ALU.add)
            cV.inc(e.tensor_copy(S3.ap()[0:64, 512:1024], S1.ap()[0:64, 0:512]))
            mil['scan1'] = cV.v
        block.vector(_)

        def _(e):
            e.wait_ge(cV.h, mil['scan1'])
            mil['seg_wr'] = ld(e, qA, seg_d.ap(), S3.ap()[0:64, 1023:1024])
        block.scalar(_)

        def _(e):
            e.wait_ge(qA.h, mil['seg_wr'])
            mil['ag_seg'] = cc.inc(e.collective_compute(
                "AllGather", ALU.bypass, replica_groups=[list(range(NCORE))],
                ins=[seg_d.ap()], outs=[segall.ap()]))
        block.gpsimd(_)

        def _(e):
            e.wait_ge(cc.h, mil['ag_seg'])
            mil['seg_ld'] = ld(e, qL, sseg[:, :], segall[:, :])
        block.sync(_)

        def _(e):
            e.wait_ge(cV.h, mil['xn_n15'])
            mil['E_mm'] = cT.inc(e.matmul(PS1[0:8, 0:64], sexpo[:, :], stdr[:, :],
                                          start=True, stop=True))
        block.tensor(_)

        def _(e):
            e.wait_ge(cT.h, mil['E_mm'])
            cS.inc(e.activation(sE[:, :], PS1[0:8, 0:64], AF.Exp))
            mil['E_exp'] = cS.v
        block.scalar(_)

        def _(e):
            e.wait_ge(cS.h, mil['E_exp'])
            e.wait_ge(qL.h, mil['seg_ld'])
            cV.inc(e.tensor_tensor(sP[:, :], sE[:, :], sseg[:, :], ALU.mult))
            mil['P'] = cV.v
        block.vector(_)

        def _(e):
            e.wait_ge(cV.h, mil['P'])
            mil['G_mm'] = cT.inc(e.matmul(PS1[0:64, 64:65], sP[:, :], sones8[:, :],
                                          start=True, stop=True))
        block.tensor(_)

        def _(e):
            e.wait_ge(cV.h, mil['scan1'])
            cG.inc(e.iota(S1.ap()[0:64, :], pattern=[[1, T]], base=1,
                          channel_multiplier=0, allow_small_or_imprecise_dtypes=True))
            mil['iota'] = cG.v
        block.gpsimd(_)

        def _(e):
            e.wait_ge(cG.h, mil['iota'])
            cV.inc(e.tensor_scalar_mul(S1.ap()[0:64, :], S1.ap()[0:64, :], stdc[:, 0:1]))
            mil['wpow_pre'] = cV.v
        block.vector(_)

        def _(e):
            e.wait_ge(cV.h, mil['wpow_pre'])
            cS.inc(e.activation(S1.ap()[0:64, :], S1.ap()[0:64, :], AF.Exp))
            mil['wpow'] = cS.v
        block.scalar(_)

        def _(e):
            e.wait_ge(cT.h, mil['G_mm'])
            e.wait_ge(cS.h, mil['wpow'])
            e.tensor_copy(sGv[:, :], PS1[0:64, 64:65])
            for _j in range(10):
                e.tensor_copy(sP[:, 0:1], sones8[:, :])
            e.tensor_scalar_mul(S1.ap()[0:64, :], S1.ap()[0:64, :], sGv[:, 0:1])
            cV.inc(e.tensor_tensor(S3.ap()[0:64, :], S3.ap()[0:64, :],
                                   S1.ap()[0:64, :], ALU.add))
            mil['scan2'] = cV.v
        block.vector(_)

        if 'states' in taps:
            def _(e):
                e.wait_ge(cV.h, mil['scan2'])
                ld(e, qA, taps['states'].ap(), S3.ap()[0:64, :])
                mil['tap_states'] = qA.v
                if 'sseg' in taps:
                    ld(e, qA, taps['sseg'].ap(), sseg[:, :])
                if 'sE' in taps:
                    ld(e, qA, taps['sE'].ap(), sE[:, :])
                if 'sP' in taps:
                    ld(e, qA, taps['sP'].ap(), sP[:, :])
                if 'sGv' in taps:
                    ld(e, qA, taps['sGv'].ap(), sGv[:, :])
            block.scalar(_)
        if 'wkv' in taps:
            def _(e):
                e.wait_ge(cV.h, mil['scan1'])
                ld(e, qA, taps['wkv'].ap(), S2.ap()[0:64, :])
            block.scalar(_)

        op_ld = [None] * 16
        op_mm = [None] * 16
        op_ev = [None] * 16
        op_st = [None] * 16
        for dc in range(16):
            def _(e, dc=dc):
                if dc < 2:
                    e.wait_ge(cV.h, mil['scan2'])
                else:
                    e.wait_ge(cV.h, op_ev[dc - 2])
                op_ld[dc] = ld(e, qL, (S0 if dc % 2 == 0 else S1)[:, :], xT.ap()[dc, :, :])
            block.sync(_)

            def _(e, dc=dc):
                if dc == 0:
                    e.wait_ge(cV.h, mil['scan2'])
                if dc >= 2:
                    e.wait_ge(cV.h, op_ev[dc - 2])
                ps = PS1 if dc % 2 == 0 else PS2
                last = None
                for tb in range(2):
                    last = e.matmul(ps[:, tb * 512:(tb + 1) * 512],
                                    WARE[0:64, 2048 + dc * 128: 2048 + (dc + 1) * 128],
                                    S3.ap()[0:64, tb * 512:(tb + 1) * 512],
                                    start=True, stop=True)
                op_mm[dc] = cT.inc(last)
            block.tensor(_)

            def _(e, dc=dc):
                e.wait_ge(cT.h, op_mm[dc])
                e.wait_ge(qL.h, op_ld[dc])
                if dc >= 2:
                    e.wait_ge(qA.h, op_st[dc - 2])
                ps = PS1 if dc % 2 == 0 else PS2
                bx = S0 if dc % 2 == 0 else S1
                bo = S2 if dc % 2 == 0 else S4
                op_ev[dc] = cV.inc(e.tensor_tensor(bo[:, :], ps[:, :], bx[:, :], ALU.add))
            block.vector(_)

            def _(e, dc=dc):
                e.wait_ge(cV.h, op_ev[dc])
                bo = S2 if dc % 2 == 0 else S4
                op_st[dc] = ld(e, qA, oa_d.ap()[dc, :, :], bo[:, :])
            block.scalar(_)

        fp_done = [None] * DCH
        for dc in range(DCH):
            def _(e, dc=dc):
                if dc == 0:
                    e.wait_ge(cV.h, op_ev[15])
                    e.wait_ge(qA.h, op_st[15])
                ch = AR1.ap()[:, dc * T:(dc + 1) * T]
                hi_b = S0b[:, 0:1024]
                lo_b = S1b[:, 0:1024]
                e.tensor_copy(hi_b, ch)
                e.tensor_tensor(lo_b, ch, hi_b, ALU.subtract)
                e.tensor_copy(AR1b[:, dc * 2048: dc * 2048 + 1024], hi_b)
                fp_done[dc] = cV.inc(
                    e.tensor_copy(AR1b[:, dc * 2048 + 1024: dc * 2048 + 2048], lo_b))
            block.vector(_)

        gu_view = wqgu_all.ap()
        pj_view = wqpj_all.ap()
        gubuf = [WARb[:, 0:4096], WARb[:, 4096:8192]]
        pjbuf = [WARb[:, 8192:10240], WARb[:, 10240:12288]]

        gu_ld = [None] * (NPAIR + 1)
        gu_mm = [None] * NPAIR
        gu_silu = [None] * NPAIR
        gu_h = [None] * NPAIR
        gu_split = [None] * NPAIR
        gu_sq = [None] * NPAIR
        gu_acc = [None] * NPAIR
        pj_ld = [[None] * 4 for _ in range(NGRP)]
        pj_mm = [[None] * 4 for _ in range(NGRP)]
        pj_ev = [[None] * 4 for _ in range(NGRP)]

        def gu_w_src(chunk):
            return gu_view[chunk // 16, chunk % 16, :, :, :]

        def _(e):
            e.wait_ge(cc.h, mil['ag_gu'])
            e.wait_ge(cT.h, op_mm[15])
            e.dma_start(out=gubuf[0][:, 0:2048].rearrange("p (a f) -> p a f", f=128),
                        in_=gu_w_src(0)).then_inc(qL.h, 16)
            qL.v += 16
            e.dma_start(out=gubuf[0][:, 2048:4096].rearrange("p (a f) -> p a f", f=128),
                        in_=gu_w_src(64)).then_inc(qL.h, 16)
            qL.v += 16
            gu_ld[0] = qL.v
        block.sync(_)

        ps_gate = [PS1, PS3]
        ps_up = [PS2, PS4]

        for g in range(NGRP):
            for pi in range(GPP):
                p = g * GPP + pi

                def _(e, p=p):
                    if p + 1 < NPAIR:
                        if p >= 1:
                            e.wait_ge(cT.h, gu_mm[p - 1])
                        e.dma_start(
                            out=gubuf[(p + 1) % 2][:, 0:2048].rearrange("p (a f) -> p a f", f=128),
                            in_=gu_w_src(p + 1)).then_inc(qL.h, 16)
                        qL.v += 16
                        e.dma_start(
                            out=gubuf[(p + 1) % 2][:, 2048:4096].rearrange("p (a f) -> p a f", f=128),
                            in_=gu_w_src(64 + p + 1)).then_inc(qL.h, 16)
                        qL.v += 16
                        gu_ld[p + 1] = qL.v
                block.sync(_)

                def _(e, p=p, g=g, pi=pi):
                    e.wait_ge(qL.h, gu_ld[p])
                    if p == 0:
                        e.wait_ge(cV.h, fp_done[DCH - 1])
                    if p >= 2:
                        e.wait_ge(cS.h, gu_silu[p - 2])
                        e.wait_ge(cV.h, gu_h[p - 2])
                    if g >= 1 and pi <= 1:
                        e.wait_ge(cV.h, pj_ev[g - 1][3])
                    wb = gubuf[p % 2]
                    pg = ps_gate[p % 2]
                    pu = ps_up[p % 2]
                    last = None
                    for tb in range(2):
                        n = 0
                        for dc in range(DCH):
                            for part in range(2):
                                rhs = AR1b[:, dc * 2048 + part * 1024 + tb * 512:
                                           dc * 2048 + part * 1024 + (tb + 1) * 512]
                                e.matmul(pg[:, tb * 512:(tb + 1) * 512],
                                         wb[:, dc * 128:(dc + 1) * 128], rhs,
                                         start=(n == 0), stop=(n == 31))
                                last = e.matmul(pu[:, tb * 512:(tb + 1) * 512],
                                                wb[:, 2048 + dc * 128: 2048 + (dc + 1) * 128],
                                                rhs, start=(n == 0), stop=(n == 31))
                                n += 1
                    gu_mm[p] = cT.inc(last)
                block.tensor(_)

                def _(e, p=p):
                    e.wait_ge(cT.h, gu_mm[p])
                    if p >= 1:
                        e.wait_ge(cV.h, gu_h[p - 1])
                    gu_silu[p] = cS.inc(e.activation(S0[:, :], ps_gate[p % 2][:, :], AF.Silu))
                block.scalar(_)

                def _(e, p=p, pi=pi):
                    e.wait_ge(cS.h, gu_silu[p])
                    if p >= 1:
                        e.wait_ge(cS.h, gu_sq[p - 1])
                    gu_h[p] = cV.inc(e.tensor_tensor(S1[:, :], S0[:, :],
                                                     ps_up[p % 2][:, :], ALU.mult))
                    hi_b = AR3b[:, pi * 2048: pi * 2048 + 1024]
                    lo_b = AR3b[:, pi * 2048 + 1024: pi * 2048 + 2048]
                    e.tensor_copy(hi_b, S1[:, :])
                    gu_split[p] = cV.inc(e.tensor_tensor(lo_b, S1[:, :], hi_b, ALU.subtract))
                block.vector(_)

                def _(e, p=p):
                    e.wait_ge(cV.h, gu_split[p])
                    if p >= 1:
                        e.wait_ge(cG.h, gu_acc[p - 1])
                    gu_sq[p] = cS.inc(e.square(S2[:, :], S1[:, :]))
                block.scalar(_)

                def _(e, p=p):
                    e.wait_ge(cS.h, gu_sq[p])
                    if p == 0:
                        if 'tap_states' in mil:
                            e.wait_ge(qA.h, mil['tap_states'])
                        gu_acc[p] = cG.inc(e.tensor_copy(S3[:, :], S2[:, :]))
                    else:
                        gu_acc[p] = cG.inc(e.tensor_tensor(S3[:, :], S3[:, :], S2[:, :], ALU.add))
                block.gpsimd(_)

            for dsub in range(4):
                def _(e, g=g, dsub=dsub):
                    if g == 0 and dsub == 0:
                        e.wait_ge(cc.h, mil['ag_pj'])
                    if dsub >= 2:
                        e.wait_ge(cT.h, pj_mm[g][dsub - 2])
                    elif g >= 1:
                        e.wait_ge(cT.h, pj_mm[g - 1][2 + dsub])
                    for k in range(GPP):
                        hc = g * GPP + k
                        src = pj_view[hc // 8, hc % 8, :, dsub * 4:(dsub + 1) * 4, :]
                        dst = pjbuf[dsub % 2][:, k * 512:(k + 1) * 512].rearrange(
                            "p (a f) -> p a f", f=128)
                        e.dma_start(out=dst, in_=src).then_inc(qL.h, 16)
                        qL.v += 16
                    pj_ld[g][dsub] = qL.v
                block.sync(_)

                def _(e, g=g, dsub=dsub):
                    e.wait_ge(qL.h, pj_ld[g][dsub])
                    if dsub == 0:
                        e.wait_ge(cS.h, gu_silu[g * GPP + GPP - 1])
                        e.wait_ge(cV.h, gu_h[g * GPP + GPP - 1])
                        e.wait_ge(cV.h, gu_split[g * GPP + GPP - 1])
                    else:
                        e.wait_ge(cV.h, pj_ev[g][dsub - 1])
                    PSL = [PS1, PS2, PS3, PS4]
                    last = None
                    for j in range(4):
                        for tb in range(2):
                            n = 0
                            for k in range(GPP):
                                for part in range(2):
                                    rhs = AR3b[:, k * 2048 + part * 1024 + tb * 512:
                                               k * 2048 + part * 1024 + (tb + 1) * 512]
                                    last = e.matmul(
                                        PSL[j][:, tb * 512:(tb + 1) * 512],
                                        pjbuf[dsub % 2][:, (k * 4 + j) * 128:(k * 4 + j + 1) * 128],
                                        rhs, start=(n == 0), stop=(n == 7))
                                    n += 1
                    pj_mm[g][dsub] = cT.inc(last)
                block.tensor(_)

                def _(e, g=g, dsub=dsub):
                    e.wait_ge(cT.h, pj_mm[g][dsub])
                    PSL = [PS1, PS2, PS3, PS4]
                    last = None
                    for j in range(4):
                        dc = dsub * 4 + j
                        dst = AR2.ap()[:, dc * T:(dc + 1) * T]
                        if g == 0:
                            last = e.tensor_copy(dst, PSL[j][:, :])
                        else:
                            last = e.tensor_tensor(dst, dst, PSL[j][:, :], ALU.add)
                    pj_ev[g][dsub] = cV.inc(last)
                block.vector(_)

        def _(e):
            e.wait_ge(cG.h, gu_acc[NPAIR - 1])
            e.wait_ge(cV.h, pj_ev[NGRP - 1][3])
            e.matmul(PS1[0:1, 0:512], sones128[:, :], S3[:, 0:512], start=True, stop=True)
            mil['rm_sums'] = cT.inc(e.matmul(
                PS1[0:1, 512:1024], sones128[:, :], S3[:, 512:1024], start=True, stop=True))
        block.tensor(_)

        emit_rows_finalize(PS1[0:1, 0:1024], None, 'rm', eps=RMS_EPS, is_rms=True,
                           nmean=8192, wait_t=lambda: mil['rm_sums'])

        def _(e):
            e.wait_ge(cV.h, mil['rm_rows'])
            e.matmul(PS2[:, 0:512], sonesrow[:, :], R_rs[:, 0:512], start=True, stop=True)
            mil['rm_bc'] = cT.inc(e.matmul(
                PS2[:, 512:1024], sonesrow[:, :], R_rs[:, 512:1024], start=True, stop=True))
        block.tensor(_)

        def _(e):
            e.wait_ge(cT.h, mil['rm_bc'])
            cV.inc(e.tensor_copy(S4[:, :], PS2[:, :]))
            mil['rsqb'] = cV.v
        block.vector(_)

        if 'F' in taps:
            def _(e):
                e.wait_ge(cV.h, pj_ev[NGRP - 1][3])
                for dcc in range(DCH):
                    ld(e, qA, taps['F'].ap()[dcc, :, :], AR2[:, dcc * T:(dcc + 1) * T])
            block.scalar(_)
        if 'xs' in taps:
            def _(e):
                e.wait_ge(cV.h, pj_ev[NGRP - 1][3])
                for dcc in range(DCH):
                    ld(e, qA, taps['xs'].ap()[dcc, :, :], xs_d.ap()[dcc, :, :])
            block.scalar(_)

        fo_ld = [None] * DCH
        fo_ev = [None] * DCH
        fo_st = [None] * DCH
        for dc in range(DCH):
            def _(e, dc=dc):
                if dc == 0:
                    e.wait_ge(cV.h, mil['rsqb'])
                if dc >= 2:
                    e.wait_ge(cV.h, fo_ev[dc - 2])
                fo_ld[dc] = ld(e, qL, (S0 if dc % 2 == 0 else S1)[:, :], oa_d.ap()[dc, :, :])
            block.sync(_)

            def _(e, dc=dc):
                e.wait_ge(qL.h, fo_ld[dc])
                if dc == 0:
                    e.wait_ge(cV.h, mil['rsqb'])
                if dc >= 2:
                    e.wait_ge(qA.h, fo_st[dc - 2])
                bx = S0 if dc % 2 == 0 else S1
                bo = AR3.ap()[:, (dc % 2) * 1024:(dc % 2 + 1) * 1024]
                e.tensor_tensor(S2[:, :], AR2[:, dc * T:(dc + 1) * T], S4[:, :], ALU.mult)
                fo_ev[dc] = cV.inc(e.tensor_tensor(bo, S2[:, :], bx[:, :], ALU.add))
            block.vector(_)

            def _(e, dc=dc):
                e.wait_ge(cV.h, fo_ev[dc])
                bo = AR3.ap()[:, (dc % 2) * 1024:(dc % 2 + 1) * 1024]
                fo_st[dc] = ld(e, qA, out_d.ap()[dc, :, :], bo)
            block.scalar(_)

        def _(e):
            e.wait_ge(qA.h, qA.v)
        block.scalar(_)

    return nc


def _stage_shared(W_shift, W_key, W_value, W_output, gate, ln_gamma, ln_beta,
                  time_decay, time_first):
    st = {}
    Ws = np.asarray(W_shift, np.float32)
    st['wsT'] = np.ascontiguousarray(Ws.reshape(16, 128, 16, 128).transpose(0, 3, 2, 1)).reshape(16, 128, 2048)
    Wk = np.asarray(W_key, np.float32)
    Wv = np.asarray(W_value, np.float32)
    st['wkT'] = np.ascontiguousarray(Wk.reshape(64, 16, 128).transpose(2, 1, 0)).reshape(128, 1024)
    st['wvT'] = np.ascontiguousarray(Wv.reshape(64, 16, 128).transpose(2, 1, 0)).reshape(128, 1024)
    Wo = np.asarray(W_output, np.float32)
    st['woT'] = np.ascontiguousarray(Wo.reshape(16, 128, 64).transpose(2, 0, 1)).reshape(64, 2048)
    st['gcols'] = np.ascontiguousarray(np.asarray(gate, np.float32).reshape(16, 128).T)
    st['lgcols'] = np.ascontiguousarray(np.asarray(ln_gamma, np.float32).reshape(16, 128).T)
    st['lbcols'] = np.ascontiguousarray(np.asarray(ln_beta, np.float32).reshape(16, 128).T)
    td = np.asarray(time_decay, np.float32)
    tf = np.asarray(time_first, np.float32)
    st['tdc'] = np.ascontiguousarray(td.reshape(64, 1))
    st['tdr'] = np.ascontiguousarray(td.reshape(1, 64))
    st['tfc'] = np.ascontiguousarray(tf.reshape(64, 1))
    ind = np.zeros((128, 2), np.float32)
    ind[0:64, 0] = 1.0
    ind[64:128, 1] = 1.0
    st['indst'] = ind.astype(bf)
    st['indbf'] = np.ascontiguousarray(ind.T)
    st['indbb'] = np.ascontiguousarray(ind.T).astype(bf)
    return st


def kernel(L0_attn_x, L0_attn_W_key, L0_attn_W_value, L0_attn_W_output,
           L0_attn_W_shift, L0_attn_gate, L0_attn_time_decay, L0_attn_time_first,
           L0_attn_ln_gamma, L0_attn_ln_beta, L0_ffn_w_gate_up, L0_ffn_w_proj):
    x = np.asarray(L0_attn_x, np.float32)[0]
    wgu = np.asarray(L0_ffn_w_gate_up, np.float32)
    wpj = np.asarray(L0_ffn_w_proj, np.float32)

    shared = _stage_shared(L0_attn_W_shift, L0_attn_W_key, L0_attn_W_value,
                           L0_attn_W_output, L0_attn_gate, L0_attn_ln_gamma,
                           L0_attn_ln_beta, L0_attn_time_decay, L0_attn_time_first)

    in_maps = []
    for c in range(NCORE):
        m = dict(shared)
        m['xT'] = np.ascontiguousarray(x[c * T:(c + 1) * T, :].T).reshape(DCH, 128, T)
        r0 = (c * T + 4096) % 8192
        m['xrT'] = np.ascontiguousarray(x[r0:r0 + T, :].T).reshape(DCH, 128, T)
        ex = np.zeros((1, 8), np.float32)
        for j in range(8):
            ex[0, j] = 1024.0 * (c - 1 - j) if j < c else 40000.0
        m['expo'] = ex
        m['qgu'] = np.ascontiguousarray(wgu[c * 2048:(c + 1) * 2048, :].T).reshape(16, 128, 2048)
        m['qpj'] = np.ascontiguousarray(wpj[:, c * 1024:(c + 1) * 1024].T).reshape(8, 128, 2048)
        in_maps.append(m)

    nc = build_program()
    res = run_bass_kernel_spmd(nc, in_maps, core_ids=list(range(NCORE)))

    out = np.empty((1, 8192, 2048), np.float32)
    for c in range(NCORE):
        o = np.asarray(res.results[c]["out"])
        out[0, c * T:(c + 1) * T, :] = o.reshape(2048, T).T
    if _DEBUG_TAPS:
        kernel._last_results = res
    return out
